# version 09
# baseline (speedup 1.0000x reference)
import contextlib
import numpy as np

import concourse.bass as bass
import concourse.mybir as mybir
from concourse.bass_utils import run_bass_kernel_spmd

F32 = mybir.dt.float32
BF16 = mybir.dt.bfloat16
AF = mybir.ActivationFunctionType
ALU = mybir.AluOpType

D = 2048
DC = 16
TOWN = 2048
NT = 4
S = 4096
NSC = 32
NFC = 64
KVH = 4
EPS_ATTN = 1e-6
EPS_LN = 1e-5
NW = float(8192 * 2048)
CBAR = float(np.float32(0.7978845608))
K1024C = float(np.float32(0.7978845608)) * 1024.0
C24 = float(2.0 ** -24)
C07 = float(np.float32(0.7))
CD7 = float(0.7 - float(np.float32(0.7)))

ENGINES = ("sync", "gpsimd", "vector", "scalar", "tensor")


class Sem:
    __slots__ = ("name", "handle", "producer", "count", "is_dma")

    def __init__(self, name, handle, producer, is_dma):
        self.name, self.handle, self.producer = name, handle, producer
        self.count, self.is_dma = 0, is_dma


def mx(*toks):
    best = None
    for t in toks:
        if t is None:
            continue
        if best is None or (t[0] is best[0] and t[1] > best[1]):
            best = t
        elif t[0] is not best[0]:
            raise ValueError("mx() needs same-sem tokens")
    return best


class Prog:

    AUTO_CHAIN = ("vector", "scalar", "gpsimd")

    def __init__(self, nc, es):
        self.nc = nc
        self.es = es
        self.ops = {e: [] for e in ENGINES}
        self.watermark = {e: {} for e in ENGINES}

    def sem(self, name, producer, is_dma=False):
        h = self.es.enter_context(self.nc.semaphore(name))
        return Sem(name, h, producer, is_dma)

    def op(self, eng, fn, waits=(), inc=None, inc_amt=1):
        waits = list(waits)
        if eng in self.AUTO_CHAIN and inc is not None and inc.count > 0:
            waits.append((inc, inc.count))
        wl = []
        for tok in waits:
            if tok is None:
                continue
            s, v = tok
            if v is None or v <= 0:
                continue
            assert v <= s.count, (
                f"deadlock: {eng} waits {s.name}>={v} but only {s.count} emitted")
            if self.watermark[eng].get(s.name, 0) >= v:
                continue
            self.watermark[eng][s.name] = v
            wl.append((s.handle, v))
        tok = None
        if inc is not None:
            assert inc.producer == eng, (eng, inc.name, inc.producer)
            inc.count += inc_amt
            tok = (inc, inc.count)
        self.ops[eng].append((fn, wl, (inc.handle, inc_amt) if inc else None))
        return tok

    def replay(self):
        with self.nc.Block() as block:
            def mk(ename):
                def go(eng):
                    for fn, waits, inc in self.ops[ename]:
                        for h, v in waits:
                            eng.wait_ge(h, v)
                        ins = eng.nop() if fn is None else fn(eng)
                        if inc is not None:
                            ins.then_inc(*inc)
                return go
            block.sync(mk("sync"))
            block.gpsimd(mk("gpsimd"))
            block.vector(mk("vector"))
            block.scalar(mk("scalar"))
            block.tensor(mk("tensor"))


_DEBUG = False


def build_nc():
    nc = bass.Bass("TRN2", target_bir_lowering=False, debug=False)
    es = contextlib.ExitStack()
    _dbg_kind = "ExternalOutput" if _DEBUG else "Internal"

    xt = nc.dram_tensor("xt", [D, S], F32, kind="ExternalInput").ap()
    wq = nc.dram_tensor("wq", [16, 128, 2048], F32, kind="ExternalInput").ap()
    wk = nc.dram_tensor("wk", [D, 512], F32, kind="ExternalInput").ap()
    wv = nc.dram_tensor("wv", [D, 512], F32, kind="ExternalInput").ap()
    wo = nc.dram_tensor("wo", [16, 128, 2048], F32, kind="ExternalInput").ap()
    w1 = nc.dram_tensor("w1", [64, 128, 2048], F32, kind="ExternalInput").ap()
    w2 = nc.dram_tensor("w2", [16, 128, 8192], F32, kind="ExternalInput").ap()
    bq = nc.dram_tensor("bq", [128, 16], F32, kind="ExternalInput").ap()
    bk = nc.dram_tensor("bk", [1, 512], F32, kind="ExternalInput").ap()
    bv = nc.dram_tensor("bv", [1, 512], F32, kind="ExternalInput").ap()
    b1 = nc.dram_tensor("b1", [128, 64], F32, kind="ExternalInput").ap()
    b2 = nc.dram_tensor("b2", [128, 16], F32, kind="ExternalInput").ap()
    out = nc.dram_tensor("out", [D, TOWN], F32, kind="ExternalOutput").ap()
    w1q = nc.dram_tensor("w1q", [64, 128, 2048], BF16, kind=_dbg_kind).ap()
    w2q = nc.dram_tensor("w2q", [16, 128, 8192], BF16, kind=_dbg_kind).ap()
    y1s = nc.dram_tensor("y1s", [NT, DC, 128, 512], F32, kind=_dbg_kind).ap()
    if _DEBUG:
        dbg_kv = nc.dram_tensor("dbg_kv", [128, 4 * 129], BF16,
                                kind="ExternalOutput").ap()
        dbg_thr = nc.dram_tensor("dbg_thr", [128, 4], F32,
                                 kind="ExternalOutput").ap()

    A = nc.alloc_sbuf_tensor("arnA", [128, 32768], BF16)
    B = nc.alloc_sbuf_tensor("arnB", [128, 32768], BF16)
    Bf = B.bitcast(F32)
    C = nc.alloc_sbuf_tensor("arnC", [128, 16384], BF16)
    Cf = C.bitcast(F32)
    E = nc.alloc_sbuf_tensor("arnE", [128, 10 * 512], F32)
    Eb = E.bitcast(BF16)
    Fst = nc.alloc_sbuf_tensor("arnF", [128, 4 * 512], F32)
    Gsq = nc.alloc_sbuf_tensor("arnG", [128, 4096], BF16)
    rows = nc.alloc_sbuf_tensor("rows", [1, 4 * 512], F32)
    kvsb = nc.alloc_sbuf_tensor("kvsb", [128, 4 * 129], BF16)
    bq_sb = nc.alloc_sbuf_tensor("bq_sb", [128, 16], F32)
    b1_sb = nc.alloc_sbuf_tensor("b1_sb", [128, 64], F32)
    b2_sb = nc.alloc_sbuf_tensor("b2_sb", [128, 16], F32)
    thr = nc.alloc_sbuf_tensor("thr", [128, 4], F32)
    acc1 = nc.alloc_sbuf_tensor("acc1", [128, 1], F32)
    acc2 = nc.alloc_sbuf_tensor("acc2", [128, 1], F32)
    acc_t = nc.alloc_sbuf_tensor("acc_t", [128, 2], F32)
    ones_m1 = nc.alloc_sbuf_tensor("ones_m1", [128, 1], F32)
    ones_k1 = nc.alloc_sbuf_tensor("ones_k1", [1, 128], F32)
    c_nhalf = nc.alloc_sbuf_tensor("c_nhalf", [1, 1], F32)
    c_none = nc.alloc_sbuf_tensor("c_none", [1, 1], F32)
    c_eps_at = nc.alloc_sbuf_tensor("c_eps_at", [1, 1], F32)
    c_eps_ln = nc.alloc_sbuf_tensor("c_eps_ln", [1, 1], F32)
    tot_sb = nc.alloc_sbuf_tensor("tot_sb", [1, 4], F32)
    PB = [nc.alloc_psum_tensor(f"PB{i}", [128, 512], F32) for i in range(8)]

    Gf = Gsq.bitcast(F32)
    Aap, Bbf, Cap, Gap = A.ap(), B.ap(), C.ap(), Gsq.ap()
    Bfap, Cfap, Gfap = Bf.ap(), Cf.ap(), Gf.ap()

    def Esl(i, n=1):
        return E.ap()[:, i * 512:(i + n) * 512]

    def row(i):
        return rows.ap()[0:1, i * 512:(i + 1) * 512]

    p = Prog(nc, es)
    NLANE = 6
    sSl = [p.sem(f"sS{i}", "sync", is_dma=True) for i in range(NLANE)]
    sGl = [p.sem(f"sG{i}", "gpsimd", is_dma=True) for i in range(NLANE)]
    sT = p.sem("sT", "tensor")
    sP = p.sem("sP", "gpsimd")
    sV = p.sem("sV", "vector")
    sC = p.sem("sC", "scalar")
    _q = {"sync": {"rr": 0, "lanes": sSl, "last": [None] * NLANE},
          "gpsimd": {"rr": 0, "lanes": sGl, "last": [None] * NLANE}}

    def dma(qeng, dst, src, waits=()):
        q = _q[qeng]
        i = q["rr"]
        q["rr"] = (i + 1) % NLANE
        tok = p.op(qeng, lambda e, a=dst, b=src: e.dma_start(out=a, in_=b),
                   waits=list(waits) + [q["last"][i]],
                   inc=q["lanes"][i], inc_amt=16)
        q["last"][i] = tok
        return tok

    st_ = {}
    for nm, t_, val in (("ones_m1", ones_m1, 1.0), ("ones_k1", ones_k1, 1.0),
                        ("c_nhalf", c_nhalf, -0.5), ("c_none", c_none, -1.0),
                        ("c_eps_at", c_eps_at, EPS_ATTN),
                        ("c_eps_ln", c_eps_ln, EPS_LN),
                        ("acc1", acc1, 0.0), ("acc2", acc2, 0.0)):
        st_[nm] = p.op("vector", lambda e, a=t_.ap(), v=val: e.memset(a, v), inc=sV)
    st_["F0"] = p.op("vector", lambda e, a=Fst.ap(): e.memset(a, 0.0), inc=sV)
    t_ones_sv = None
    for par in range(2):
        for h in range(KVH):
            c0 = 1024 + par * 520 + h * 129 + 128
            t_ones_sv = p.op("vector", lambda e, a=Gap[:, c0:c0 + 1]:
                             e.memset(a, 1.0), inc=sV)
    t_bq = dma("sync", bq_sb.ap(), bq)
    t_b1 = dma("sync", b1_sb.ap(), b1)
    t_b2 = dma("sync", b2_sb.ap(), b2)
    t_bkr = dma("sync", row(0), bk)
    t_bvr = dma("sync", row(1), bv)
    t_bkmm = p.op("tensor", lambda e: e.matmul(PB[6].ap(), ones_k1.ap(), row(0),
                                               start=True, stop=True),
                  waits=[t_bkr, st_["ones_k1"]], inc=sT)
    t_bvmm = p.op("tensor", lambda e: e.matmul(PB[7].ap(), ones_k1.ap(), row(1),
                                               start=True, stop=True),
                  waits=[t_bvr], inc=sT)
    bkbc, bvbc = Esl(3), Esl(4)
    t_bkbc = p.op("vector", lambda e: e.tensor_copy(bkbc, PB[6].ap()),
                  waits=[t_bkmm], inc=sV)
    t_bvbc = p.op("vector", lambda e: e.tensor_copy(bvbc, PB[7].ap()),
                  waits=[t_bvmm], inc=sV)

    t_xown, t_xpart = [], []
    for dc in range(DC):
        t_xown.append(dma("gpsimd", Aap[:, dc * 2048:(dc + 1) * 2048],
                          xt[dc * 128:(dc + 1) * 128, 0:TOWN]))
    wkT = Cap[:, 0:8192].rearrange("p (dc e) -> p dc e", e=512)
    wvT = Cap[:, 8192:16384].rearrange("p (dc e) -> p dc e", e=512)
    t_wk = dma("gpsimd", wkT, wk.rearrange("(dc p) e -> p dc e", p=128))
    t_wv = dma("gpsimd", wvT, wv.rearrange("(dc p) e -> p dc e", p=128))
    for dc in range(DC):
        t_xpart.append(dma("gpsimd", Bbf[:, dc * 2048:(dc + 1) * 2048],
                           xt[dc * 128:(dc + 1) * 128, TOWN:S]))

    scale_jobs = [("w1", w1[fc, :, h2 * 1024:(h2 + 1) * 1024])
                  for fc in range(64) for h2 in range(2)]
    scale_jobs += [("w2", w2[dcq, :, e8 * 1024:(e8 + 1) * 1024])
                   for dcq in range(16) for e8 in range(8)]
    Eb9 = Eb.ap()[:, 9 * 1024:10 * 1024]
    sst = {"i": 0, "abs": [None, None],
           "acc": {"w1": st_["acc1"], "w2": st_["acc2"]}, "done": False}

    def emit_scale(njobs):
        for _ in range(njobs):
            i = sst["i"]
            if i >= len(scale_jobs):
                break
            sst["i"] += 1
            which, src = scale_jobs[i]
            sl = i % 2
            stage = Esl(5 + 2 * sl, 2)
            acol = acc_t.ap()[:, sl:sl + 1]
            tok_d = dma("sync", stage, src, waits=[sst["abs"][sl]])
            tok_a = p.op("scalar",
                         lambda e, s2=stage, a2=acol:
                         e.activation(Eb9, s2, AF.Abs, accum_out=a2),
                         waits=[tok_d, sst["acc"]["w1"], sst["acc"]["w2"]],
                         inc=sC)
            sst["abs"][sl] = tok_a
            accd = acc1.ap() if which == "w1" else acc2.ap()
            sst["acc"][which] = p.op(
                "vector", lambda e, a2=accd, c2=acol:
                e.scalar_tensor_tensor(a2, c2, K1024C, a2, ALU.subtract, ALU.add),
                waits=[tok_a], inc=sV)
        if sst["i"] >= len(scale_jobs):
            sst["done"] = True

    kfree = [None, None]
    vfree = [None, None]
    skvfree = [None, None]
    pend_kv = None
    for sc in range(NSC):
        par = sc % 2
        own = sc < 16
        base = (sc if own else sc - 16) * 128
        srcA = Aap if own else Bbf
        xw = (t_xown if own else t_xpart)
        xtiles = [srcA[:, dc * 2048 + base: dc * 2048 + base + 128]
                  for dc in range(DC)]
        tk = tv = None
        for dc in range(DC):
            tk = p.op("tensor",
                      lambda e, o=PB[par].ap(), l=xtiles[dc], r=wkT[:, dc, :],
                      s=(dc == 0), z=(dc == DC - 1):
                      e.matmul(o, l, r, start=s, stop=z),
                      waits=[xw[dc], t_wk, kfree[par]],
                      inc=sT if dc == DC - 1 else None)
        if pend_kv is not None:
            pend_kv()
            pend_kv = None
        for dc in range(DC):
            tv = p.op("tensor",
                      lambda e, o=PB[2 + par].ap(), l=xtiles[dc], r=wvT[:, dc, :],
                      s=(dc == 0), z=(dc == DC - 1):
                      e.matmul(o, l, r, start=s, stop=z),
                      waits=[t_wv, vfree[par]],
                      inc=sT if dc == DC - 1 else None)
        sk_t = Gap[:, par * 512:(par + 1) * 512]
        svb = 1024 + par * 520
        t1 = p.op("vector", lambda e, o=Esl(0), a=PB[par].ap(), b=bkbc:
                  e.tensor_add(o, a, b), waits=[tk, t_bkbc], inc=sV)
        t2 = p.op("vector", lambda e, o=Esl(1), a=Esl(0):
                  e.tensor_scalar_min(o, a, 0.0), inc=sV)
        t3 = p.op("vector", lambda e, o=Esl(2), a=Esl(0):
                  e.tensor_scalar_max(o, a, 0.0), inc=sV)
        kfree[par] = t3
        t4 = p.op("scalar", lambda e, o=Esl(1), a=Esl(1):
                  e.activation(o, a, AF.Exp), waits=[t2], inc=sC)
        t5 = p.op("vector", lambda e, o=sk_t, a=Esl(1), b=Esl(2):
                  e.tensor_add(o, a, b), waits=[t4, skvfree[par]], inc=sV)
        t6 = None
        for h in range(KVH):
            t6 = p.op("vector",
                      lambda e, o=Gap[:, svb + h * 129: svb + h * 129 + 128],
                      a=PB[2 + par].ap()[:, h * 128:(h + 1) * 128],
                      b=bvbc[:, h * 128:(h + 1) * 128]:
                      e.scalar_tensor_tensor(o, a, 1.0, b, ALU.mult, ALU.add),
                      waits=[tv, t_bvbc], inc=sV)
        vfree[par] = t6

        def mk_kv(sc=sc, par=par, sk_t=sk_t, svb=svb, t5=t5, t6=t6):
            tkv = None
            for h in range(KVH):
                tkv = p.op("tensor",
                           lambda e, o=PB[4 + h].ap()[:, 0:129],
                           l=sk_t[:, h * 128:(h + 1) * 128],
                           r=Gap[:, svb + h * 129: svb + h * 129 + 129],
                           s=(sc == 0), z=(sc == NSC - 1):
                           e.matmul(o, l, r, start=s, stop=z),
                           waits=[t5, t6, t_ones_sv],
                           inc=sT if h == KVH - 1 else None)
            skvfree[par] = tkv
            return tkv
        pend_kv = mk_kv
        emit_scale(6)
    last_kv_mm = pend_kv()

    t_kv_ev = None
    for h in range(KVH):
        t_kv_ev = p.op("vector",
                       lambda e, o=kvsb.ap()[:, h * 129:(h + 1) * 129],
                       a=PB[4 + h].ap()[:, 0:129]: e.tensor_copy(o, a),
                       waits=[last_kv_mm], inc=sV)
    if _DEBUG:
        dma("gpsimd", dbg_kv, kvsb.ap(), waits=[t_kv_ev])

    def wq_buf(i):
        return Cap[:, (i % 2) * 2048:((i % 2) + 1) * 2048]

    def wo_buf(i):
        return Cap[:, 4096 + (i % 2) * 2048: 4096 + ((i % 2) + 1) * 2048]

    wq_tok = [None] * 16
    for i in range(2):
        wq_tok[i] = dma("gpsimd", wq_buf(i), wq[i], waits=[last_kv_mm])

    def sq_tile(ic, t):
        return Gap[:, (ic % 2) * 2048 + t * 512: (ic % 2) * 2048 + (t + 1) * 512]

    q_free = [None, None]
    qkv_free = [None, None]
    nrm_free = [None, None]
    rbc_free = None
    attnT_tok = [None] * 16
    qkv_last_mm = [None] * 16
    sq_toks = [[None] * NT for _ in range(16)]
    uro = [None, None]
    last_q_mm = [None] * 16
    thr_done = None

    qkv_pend = []

    def emit_qkv_B():
        nonlocal rbc_free
        if not qkv_pend:
            return
        ic, t, qp, j, tqkv, tr = qkv_pend.pop(0)
        tbc = p.op("tensor", lambda e, o=PB[4].ap(), l=ones_k1.ap(),
                   r=row(2 + j):
                   e.matmul(o, l, r, start=True, stop=True),
                   waits=[tr, rbc_free], inc=sT)
        uro[j] = tbc
        tcp = p.op("vector", lambda e, o=Esl(0), a=PB[4].ap():
                   e.tensor_copy(o, a), waits=[tbc], inc=sV)
        rbc_free = tcp
        tdv = p.op("vector",
                   lambda e,
                   o=Bbf[:, ic * 2048 + t * 512: ic * 2048 + (t + 1) * 512],
                   a=PB[2 if qp == 0 else 5].ap(), b=Esl(0):
                   e.tensor_mul(o, a, b), waits=[tqkv], inc=sV)
        qkv_free[qp] = tdv
        if t == NT - 1:
            attnT_tok[ic] = tdv

    def emit_qkv(ic, t):
        kvh = ic // 4
        kv_l = kvsb.ap()[:, kvh * 129: kvh * 129 + 128]
        ks_l = kvsb.ap()[:, kvh * 129 + 128: kvh * 129 + 129]
        sq_t = sq_tile(ic, t)
        qp = (ic * NT + t) % 2
        tqkv = p.op("tensor", lambda e, o=PB[2 if qp == 0 else 5].ap(),
                    l=kv_l, r=sq_t:
                    e.matmul(o, l, r, start=True, stop=True),
                    waits=[sq_toks[ic][t], t_kv_ev, qkv_free[qp]], inc=sT)
        tnrm = p.op("tensor",
                    lambda e, o=PB[3 if qp == 0 else 6].ap()[0:1, :],
                    l=ks_l, r=sq_t:
                    e.matmul(o, l, r, start=True, stop=True),
                    waits=[nrm_free[qp]], inc=sT)
        qkv_last_mm[ic] = tnrm
        j = (ic * NT + t) % 2
        u_r, r_r = row(0 + j), row(2 + j)
        tu = p.op("scalar", lambda e, o=u_r,
                  a=PB[3 if qp == 0 else 6].ap()[0:1, :],
                  b=c_eps_at.ap():
                  e.activation(o, a, AF.Ln, bias=b),
                  waits=[tnrm, uro[j], st_["c_eps_at"],
                         t_bkr if ic == 0 else None,
                         t_bvr if ic == 0 else None], inc=sC)
        nrm_free[qp] = tu
        tr = p.op("scalar", lambda e, o=r_r, a=u_r, s2=c_none.ap():
                  e.activation(o, a, AF.Exp, scale=s2),
                  waits=[st_["c_none"]], inc=sC)
        qkv_pend.append((ic, t, qp, j, tqkv, tr))

    for ic in range(16):
        for t in range(NT):
            par = (ic * NT + t) % 2
            tq = None
            for dc in range(DC):
                tq = p.op("tensor",
                          lambda e, o=PB[par].ap(),
                          l=wq_buf(ic)[:, dc * 128:(dc + 1) * 128],
                          r=Aap[:, dc * 2048 + t * 512: dc * 2048 + (t + 1) * 512],
                          s=(dc == 0), z=(dc == DC - 1):
                          e.matmul(o, l, r, start=s, stop=z),
                          waits=[wq_tok[ic], q_free[par],
                                 qkv_last_mm[ic - 2] if (t == 0 and ic >= 2)
                                 else None],
                          inc=sT if dc == DC - 1 else None)
            bqc = bq_sb.ap()[:, ic:ic + 1]
            m1_ = p.op("vector", lambda e, o=Esl(1), a=PB[par].ap(), b=bqc:
                       e.tensor_scalar(o, a, b, 0.0, ALU.add, ALU.min),
                       waits=[tq, t_bq], inc=sV)
            m2_ = p.op("vector", lambda e, o=Esl(2), a=PB[par].ap(), b=bqc:
                       e.tensor_scalar(o, a, b, 0.0, ALU.add, ALU.max),
                       inc=sV)
            q_free[par] = m2_
            m3_ = p.op("scalar", lambda e, o=Esl(1), a=Esl(1):
                       e.activation(o, a, AF.Exp), waits=[m1_], inc=sC)
            m4_ = p.op("vector", lambda e, o=sq_tile(ic, t), a=Esl(1), b=Esl(2):
                       e.tensor_add(o, a, b), waits=[m3_], inc=sV)
            sq_toks[ic][t] = m4_
            emit_qkv_B()
            if ic >= 1:
                emit_qkv(ic - 1, t)
        last_q_mm[ic] = tq
        if ic + 2 < 16:
            wq_tok[ic + 2] = dma("gpsimd", wq_buf(ic + 2), wq[ic + 2],
                                 waits=[tq])
        emit_scale(8)
        if sst["done"] and thr_done is None:
            slk0 = p.op("vector", lambda e, o=Fst.ap()[:, 0:512]:
                        e.tensor_copy(o, o),
                        waits=[sst["acc"]["w1"], sst["acc"]["w2"], st_["F0"]],
                        inc=sV)
            tw1 = p.op("tensor", lambda e, o=PB[7].ap()[0:1, 0:1],
                       l=ones_m1.ap(), r=acc1.ap():
                       e.matmul(o, l, r, start=True, stop=True),
                       waits=[slk0, t_kv_ev, st_["ones_m1"]], inc=sT)
            tw2 = p.op("tensor", lambda e, o=PB[7].ap()[0:1, 4:5],
                       l=ones_m1.ap(), r=acc2.ap():
                       e.matmul(o, l, r, start=True, stop=True), inc=sT)
            p.op("vector", lambda e, o=tot_sb.ap()[0:1, 0:1],
                 a=PB[7].ap()[0:1, 0:1]:
                 e.tensor_scalar(o, a, C24, CBAR, ALU.mult, ALU.add),
                 waits=[tw1, tw2], inc=sV)
            p.op("vector", lambda e, o=tot_sb.ap()[0:1, 1:2],
                 a=PB[7].ap()[0:1, 4:5]:
                 e.tensor_scalar(o, a, C24, CBAR, ALU.mult, ALU.add), inc=sV)
            p.op("vector", lambda e, o=Fst.ap()[:, 0:512]: e.tensor_copy(o, o),
                 inc=sV)
            p.op("vector", lambda e, o=acc_t.ap()[0:1, 0:1],
                 a=tot_sb.ap()[0:1, 0:1]: e.tensor_scalar_mul(o, a, C07),
                 inc=sV)
            p.op("vector", lambda e, o=acc_t.ap()[0:1, 1:2],
                 a=tot_sb.ap()[0:1, 1:2]: e.tensor_scalar_mul(o, a, C07),
                 inc=sV)
            p.op("vector", lambda e, o=tot_sb.ap()[0:1, 2:3],
                 a=tot_sb.ap()[0:1, 0:1]: e.tensor_scalar_mul(o, a, CD7),
                 inc=sV)
            p.op("vector", lambda e, o=tot_sb.ap()[0:1, 3:4],
                 a=tot_sb.ap()[0:1, 1:2]: e.tensor_scalar_mul(o, a, CD7),
                 inc=sV)
            p.op("vector", lambda e, o=Fst.ap()[:, 0:512]: e.tensor_copy(o, o),
                 inc=sV)
            p.op("vector", lambda e, o=tot_sb.ap()[0:1, 1:2],
                 a=acc_t.ap()[0:1, 0:1], b=tot_sb.ap()[0:1, 2:3]:
                 e.tensor_add(o, a, b), inc=sV)
            p.op("vector", lambda e, o=tot_sb.ap()[0:1, 3:4],
                 a=acc_t.ap()[0:1, 1:2], b=tot_sb.ap()[0:1, 3:4]:
                 e.tensor_add(o, a, b), inc=sV)
            p.op("vector", lambda e, o=Fst.ap()[:, 0:512]: e.tensor_copy(o, o),
                 inc=sV)
            p.op("vector", lambda e, o=tot_sb.ap()[0:1, 0:1],
                 a=tot_sb.ap()[0:1, 1:2]: e.tensor_scalar_mul(o, a, -1.0),
                 inc=sV)
            p.op("vector", lambda e, o=tot_sb.ap()[0:1, 2:3],
                 a=tot_sb.ap()[0:1, 3:4]: e.tensor_scalar_mul(o, a, -1.0),
                 inc=sV)
            p.op("vector", lambda e, o=Fst.ap()[:, 0:512]: e.tensor_copy(o, o),
                 inc=sV)
            slk1 = p.op("vector", lambda e, o=Fst.ap()[:, 0:512]:
                        e.tensor_copy(o, o), inc=sV)
            tbc2 = p.op("tensor", lambda e, o=PB[7].ap()[:, 8:12],
                        l=ones_k1.ap(), r=tot_sb.ap()[0:1, 0:4]:
                        e.matmul(o, l, r, start=True, stop=True),
                        waits=[slk1], inc=sT)
            thr_done = p.op("vector", lambda e, o=thr.ap(),
                            a=PB[7].ap()[:, 8:12]:
                            e.tensor_copy(o, a), waits=[tbc2], inc=sV)
            if _DEBUG:
                dma("gpsimd", dbg_thr, thr.ap(), waits=[thr_done])
    for t in range(NT):
        emit_qkv_B()
        emit_qkv(15, t)
    emit_qkv_B()
    assert thr_done is not None

    def Ssl(t):
        return Fst.ap()[:, (2 * t) * 512:(2 * t + 1) * 512] if t < 2 \
            else Esl(5 + 2 * (t - 2))

    def Sqsl(t):
        return Fst.ap()[:, (2 * t + 1) * 512:(2 * t + 2) * 512] if t < 2 \
            else Esl(6 + 2 * (t - 2))

    stz = [None] * NT
    for t in range(NT):
        w8 = [st_["F0"]]
        if t >= 2:
            w8 = [mx(sst["abs"][0], sst["abs"][1]), thr_done]
        p.op("vector", lambda e, o=Ssl(t): e.memset(o, 0.0), waits=w8, inc=sV)
        stz[t] = p.op("vector", lambda e, o=Sqsl(t): e.memset(o, 0.0), inc=sV)

    wo_tok = [None] * 16
    for i in range(2):
        wo_tok[i] = dma("gpsimd", wo_buf(i), wo[i], waits=[last_kv_mm])
    wo_free = [None, None]
    xres_tok = [None, None]
    xres_free = [None, None]
    y_free = [None, None]
    y1_wr = [[None] * DC for _ in range(NT)]
    wo_mm_done = None
    sqprev = None
    for dc in range(DC):
        for t in range(NT):
            par = (dc * NT + t) % 2
            two = None
            for icc in range(16):
                two = p.op("tensor",
                           lambda e, o=PB[5 + par].ap(),
                           l=wo_buf(dc)[:, icc * 128:(icc + 1) * 128],
                           r=Bbf[:, icc * 2048 + t * 512: icc * 2048 + (t + 1) * 512],
                           s=(icc == 0), z=(icc == 15):
                           e.matmul(o, l, r, start=s, stop=z),
                           waits=[wo_tok[dc], attnT_tok[icc], wo_free[par],
                                  thr_done if (dc == 0 and t == 0) else None,
                                  attnT_tok[15] if (dc == 0 and t < 2)
                                  else None],
                           inc=sT if icc == 15 else None)
            wo_mm_done = two
            xs = (dc * NT + t) % 2
            xres = Esl(3 + xs)
            xres_tok[xs] = dma("gpsimd", xres,
                               xt[dc * 128:(dc + 1) * 128, t * 512:(t + 1) * 512],
                               waits=[xres_free[xs],
                                      t_bkbc if (dc == 0 and t < 2) else None,
                                      t_bvbc if (dc == 0 and t < 2) else None])
            ys = (dc * NT + t) % 2
            ybuf = Esl(0 + ys)
            ty = p.op("vector", lambda e, o=ybuf, a=PB[5 + par].ap(), b=xres:
                      e.tensor_add(o, a, b),
                      waits=[two, xres_tok[xs], y_free[ys]], inc=sV)
            xres_free[xs] = ty
            wo_free[par] = ty
            y1_wr[t][dc] = dma("gpsimd", y1s[t, dc], ybuf, waits=[ty])
            y_free[ys] = y1_wr[t][dc]
            p.op("vector", lambda e, o=Ssl(t), a=ybuf:
                 e.tensor_add(o, o, a), waits=[stz[t]], inc=sV)
            tsq = p.op("scalar", lambda e, o=Esl(2), a=ybuf:
                       e.activation(o, a, AF.Square),
                       waits=[ty, sqprev], inc=sC)
            sqprev = p.op("vector", lambda e, o=Sqsl(t), a=Esl(2):
                          e.tensor_add(o, o, a), waits=[tsq], inc=sV)
        if dc + 2 < 16:
            wo_tok[dc + 2] = dma("gpsimd", wo_buf(dc + 2), wo[dc + 2],
                                 waits=[wo_mm_done])

    def Abc1(t):
        return Fst.ap()[:, (2 * t) * 512:(2 * t + 1) * 512] if t < 2 \
            else Gfap[:, (2 * (t - 2)) * 512:(2 * (t - 2) + 1) * 512]

    def Bbc1(t):
        return Fst.ap()[:, (2 * t + 1) * 512:(2 * t + 2) * 512] if t < 2 \
            else Gfap[:, (2 * (t - 2) + 1) * 512:(2 * (t - 2) + 2) * 512]

    ab_done = [None] * NT
    st_free = None
    kv_slk = kvsb.ap()
    for t in range(NT):
        trs = p.op("tensor", lambda e, o=PB[7].ap()[0:1, :], l=ones_m1.ap(),
                   r=Ssl(t): e.matmul(o, l, r, start=True, stop=True),
                   waits=[sqprev, st_free], inc=sT)
        tm = p.op("vector", lambda e, o=row(0), a=PB[7].ap()[0:1, :]:
                  e.tensor_scalar_mul(o, a, 1.0 / 2048.0),
                  waits=[trs], inc=sV)
        trq = p.op("tensor", lambda e, o=PB[7].ap()[0:1, :], l=ones_m1.ap(),
                   r=Sqsl(t): e.matmul(o, l, r, start=True, stop=True),
                   waits=[tm], inc=sT)
        te2 = p.op("vector", lambda e, o=row(1), a=PB[7].ap()[0:1, :]:
                   e.tensor_scalar_mul(o, a, 1.0 / 2048.0),
                   waits=[trq], inc=sV)
        tm2 = p.op("scalar", lambda e, o=row(2), a=row(0):
                   e.activation(o, a, AF.Square), waits=[tm], inc=sC)
        tvv = p.op("vector", lambda e, o=row(1), a=row(1), b=row(2):
                   e.tensor_sub(o, a, b), waits=[tm2], inc=sV)
        tl = p.op("scalar", lambda e, o=row(2), a=row(1), b=c_eps_ln.ap():
                  e.activation(o, a, AF.Ln, bias=b),
                  waits=[tvv, st_["c_eps_ln"]], inc=sC)
        tA = p.op("scalar", lambda e, o=row(3), a=row(2), s2=c_nhalf.ap():
                  e.activation(o, a, AF.Exp, scale=s2),
                  waits=[st_["c_nhalf"]], inc=sC)
        tB1 = p.op("vector", lambda e, o=row(2), a=row(0), b=row(3):
                   e.tensor_mul(o, a, b), waits=[tA, tl], inc=sV)
        p.op("vector", lambda e, o=row(2), a=row(2):
             e.tensor_scalar_mul(o, a, -1.0), inc=sV)
        tsl = p.op("vector", lambda e, o=kv_slk: e.tensor_copy(o, o),
                   waits=[qkv_last_mm[15]], inc=sV)
        tba = p.op("tensor", lambda e, o=PB[7].ap(), l=ones_k1.ap(), r=row(3):
                   e.matmul(o, l, r, start=True, stop=True),
                   waits=[tA, tsl], inc=sT)
        tca = p.op("vector", lambda e, o=Abc1(t), a=PB[7].ap():
                   e.tensor_copy(o, a), waits=[tba, sqprev], inc=sV)
        tbb = p.op("tensor", lambda e, o=PB[7].ap(), l=ones_k1.ap(), r=row(2):
                   e.matmul(o, l, r, start=True, stop=True),
                   waits=[tca, tsl], inc=sT)
        tcb = p.op("vector", lambda e, o=Bbc1(t), a=PB[7].ap():
                   e.tensor_copy(o, a), waits=[tbb], inc=sV)
        ab_done[t] = tcb
        st_free = tcb

    S2N, SQ2N = Esl(3), Esl(4)
    h_free = [None, None]
    out_free = [None, None]
    bc_free = None
    st2_free = None
    w1q_tok = [None, None]
    w1q_free = [None, None]
    w2q_tok = [None, None, None]
    w2q_free = [None, None, None]
    x1_ready = [None] * DC
    x1bf_ready = [None] * DC
    x1_free = [None] * DC
    hbf_tok = [None] * NFC
    out_last = None
    qres = {"rr": 0, "stage_free": [None] * 4, "b_free": None,
            "wr_w1": [[None] * 4 for _ in range(64)],
            "wr_w2": [[None] * 16 for _ in range(16)], "lastB": None,
            "lastV": None, "usedC": False, "usedV": False}
    gwr_w1 = [[], []]
    gwr_w2 = [[], [], []]
    stats_guard = ab_done[3]
    abs_guard = mx(sst["abs"][0], sst["abs"][1])

    def quant_piece(which, idx, pc512, dst, dram_dst, dst_waits):
        first = qres["rr"] == 0
        sl = qres["rr"] % 4
        qres["rr"] += 1
        stage = Esl(5 + sl)
        if which == "w1":
            srcw = w1[idx, :, pc512 * 512:(pc512 + 1) * 512]
            tpos, tneg = thr.ap()[:, 1:2], thr.ap()[:, 0:1]
            cform = (pc512 % 4) == 1
        else:
            srcw = w2[idx, :, pc512 * 512:(pc512 + 1) * 512]
            tpos, tneg = thr.ap()[:, 3:4], thr.ap()[:, 2:3]
            cform = (pc512 % 8) in (1, 4, 6)
        tok_d = dma("sync", stage, srcw,
                    waits=[qres["stage_free"][sl],
                           stats_guard if first else None])
        if cform:
            firstC = not qres["usedC"]
            qres["usedC"] = True
            s1 = Eb.ap()[:, 2048:2560]
            s2 = Eb.ap()[:, 2560:3072]
            ts1 = p.op("scalar", lambda e, o=s1, a=stage, b=tneg:
                       e.activation(o, a, AF.Sign, bias=b),
                       waits=[tok_d, thr_done,
                              sqprev if firstC else None], inc=sC)
            ts2 = p.op("scalar", lambda e, o=s2, a=stage, b=tpos:
                       e.activation(o, a, AF.Sign, bias=b), inc=sC)
            qres["stage_free"][sl] = ts2
            tad = p.op("vector", lambda e, o=s1, a=s1, b=s2:
                       e.tensor_add(o, a, b), waits=[ts2], inc=sV)
            tqv = p.op("scalar", lambda e, o=dst, a=s1:
                       e.activation(o, a, AF.Sign),
                       waits=[tad] + list(dst_waits), inc=sC)
            qres["lastB"] = tqv
        else:
            firstV = not qres["usedV"]
            qres["usedV"] = True
            btile = Esl(9)
            tb = p.op("vector", lambda e, o=btile, a=stage, b2=tneg:
                      e.tensor_single_scalar(o, a, b2, ALU.is_le),
                      waits=[tok_d, thr_done, qres["b_free"],
                             abs_guard if firstV else None], inc=sV)
            tqv = p.op("vector", lambda e, o=dst, a=stage, s2=tpos, b2=btile:
                       e.scalar_tensor_tensor(o, a, s2, b2,
                                              ALU.is_ge, ALU.subtract),
                       waits=[tb] + list(dst_waits), inc=sV)
            qres["stage_free"][sl] = tqv
            qres["b_free"] = tqv
            qres["lastV"] = tqv
        tw = dma("gpsimd", dram_dst, dst, waits=[tqv])
        return tqv, tw, cform

    def w1q_buf(i):
        return Cap[:, (i % 2) * 2048:((i % 2) + 1) * 2048]

    def w2q_slot(i):
        return Cap[:, 4096 + (i % 3) * 4096: 4096 + ((i % 3) + 1) * 4096]

    def emit_ln1_dc(t, dc):
        x1d = Bfap[:, dc * 512:(dc + 1) * 512]
        x1b = Bbf[:, 16384 + dc * 512: 16384 + (dc + 1) * 512]
        tld = dma("sync", x1d, y1s[t, dc],
                  waits=[y1_wr[t][dc], x1_free[dc],
                         wo_mm_done if t == 0 else None])
        ta1 = p.op("vector", lambda e, o=Esl(1), a=x1d, b=Abc1(t):
                   e.tensor_mul(o, a, b),
                   waits=[tld, ab_done[t],
                          y_free[0] if (t == 0 and dc == 0) else None,
                          y_free[1] if (t == 0 and dc == 0) else None,
                          ln1tmp_free[0] if (t > 0 and dc == 0) else None],
                   inc=sV)
        ta2 = p.op("vector", lambda e, o=x1d, a=Esl(1), b=Bbc1(t):
                   e.tensor_add(o, a, b), inc=sV)
        x1_ready[dc] = ta2
        x1bf_ready[dc] = p.op("vector", lambda e, o=x1b, a=x1d:
                              e.tensor_copy(o, a), inc=sV)

    ln1tmp_free = [None]
    ytmp_free = None
    for t in range(NT):
        for dc in (range(DC) if t == 0 else []):
            emit_ln1_dc(t, dc)
        for fc in range(NFC):
            buf = w1q_buf(fc)
            par = fc % 2
            if t == 0:
                wtoks = []
                vlast = clast = None
                for pc in range(4):
                    dw = [w1q_free[par]] + gwr_w1[par]
                    if fc < 2:
                        dw.append(last_q_mm[15])
                    tqv, tw, cf = quant_piece(
                        "w1", fc, pc, buf[:, pc * 512:(pc + 1) * 512],
                        w1q[fc, :, pc * 512:(pc + 1) * 512], dw)
                    qres["wr_w1"][fc][pc] = tw
                    wtoks.append(tw)
                    if cf:
                        clast = tqv
                    else:
                        vlast = tqv
                w1q_tok[par] = [x for x in (vlast, clast) if x is not None]
                gwr_w1[par] = wtoks
            else:
                w1q_tok[par] = [dma(
                    "sync", buf, w1q[fc],
                    waits=[w1q_free[par]] + qres["wr_w1"][fc] + gwr_w1[par])]
                gwr_w1[par] = []
            th = None
            for dc in range(DC):
                th = p.op("tensor",
                          lambda e, o=PB[par].ap(),
                          l=buf[:, dc * 128:(dc + 1) * 128],
                          r=Bbf[:, 16384 + dc * 512: 16384 + (dc + 1) * 512],
                          s=(dc == 0), z=(dc == DC - 1):
                          e.matmul(o, l, r, start=s, stop=z),
                          waits=w1q_tok[par] + [x1bf_ready[dc],
                                                h_free[par]],
                          inc=sT if dc == DC - 1 else None)
            w1q_free[par] = th
            hb = Aap[:, fc * 512:(fc + 1) * 512]
            hbf_tok[fc] = p.op("scalar",
                               lambda e, o=hb, a=PB[par].ap(),
                               b=b1_sb.ap()[:, fc:fc + 1]:
                               e.activation(o, a, AF.Gelu, bias=b),
                               waits=[th, t_b1], inc=sC)
            h_free[par] = hbf_tok[fc]
        tacc = None
        for dc in range(DC):
            halves = []
            for hh in range(2):
                slot = (dc * 2 + hh) % 3
                sap = w2q_slot(dc * 2 + hh)
                if t == 0:
                    wtoks = []
                    vlast = clast = None
                    for pc in range(8):
                        dw = [w2q_free[slot]] + gwr_w2[slot]
                        if dc < 2:
                            dw.append(wo_mm_done)
                        tqv, tw, cf = quant_piece(
                            "w2", dc, hh * 8 + pc,
                            sap[:, pc * 512:(pc + 1) * 512],
                            w2q[dc, :, hh * 4096 + pc * 512:
                                hh * 4096 + (pc + 1) * 512], dw)
                        qres["wr_w2"][dc][hh * 8 + pc] = tw
                        wtoks.append(tw)
                        if cf:
                            clast = tqv
                        else:
                            vlast = tqv
                    w2q_tok[slot] = [x for x in (vlast, clast)
                                     if x is not None]
                    gwr_w2[slot] = wtoks
                else:
                    w2q_tok[slot] = [dma(
                        "sync", sap, w2q[dc, :, hh * 4096:(hh + 1) * 4096],
                        waits=[w2q_free[slot]] + gwr_w2[slot]
                              + qres["wr_w2"][dc][hh * 8: hh * 8 + 8])]
                    gwr_w2[slot] = []
                halves.append((slot, sap, w2q_tok[slot]))
            par = dc % 2
            tmm = None
            for fc in range(NFC):
                slot, sap, wtok = halves[fc // 32]
                tmm = p.op("tensor",
                           lambda e, o=PB[2 + par].ap(),
                           l=sap[:, (fc % 32) * 128:((fc % 32) + 1) * 128],
                           r=Aap[:, fc * 512:(fc + 1) * 512],
                           s=(fc == 0), z=(fc == NFC - 1):
                           e.matmul(o, l, r, start=s, stop=z),
                           waits=wtok + [hbf_tok[fc], out_free[par]],
                           inc=sT if fc % 32 == 31 else None)
                if fc == 31:
                    w2q_free[halves[0][0]] = tmm
            w2q_free[halves[1][0]] = tmm
            x1d = Bfap[:, dc * 512:(dc + 1) * 512]
            tv1 = p.op("scalar", lambda e, o=Esl(0), a=PB[2 + par].ap(),
                       b=b2_sb.ap()[:, dc:dc + 1]:
                       e.activation(o, a, AF.Identity, bias=b),
                       waits=[tmm, t_b2, ytmp_free], inc=sC)
            out_free[par] = tv1
            tv2 = p.op("vector", lambda e, o=x1d, a=Esl(0), b=x1d:
                       e.tensor_add(o, a, b), waits=[x1_ready[dc], tv1],
                       inc=sV)
            ytmp_free = tv2
            if dc == 0:
                p.op("vector", lambda e, o=S2N: e.memset(o, 0.0),
                     waits=[xres_tok[0] if t == 0 else None], inc=sV)
                p.op("vector", lambda e, o=SQ2N: e.memset(o, 0.0),
                     waits=[xres_tok[1] if t == 0 else None], inc=sV)
            p.op("vector", lambda e, o=S2N, a=x1d:
                 e.tensor_add(o, o, a), waits=[tv2], inc=sV)
            t2s = p.op("scalar", lambda e, o=Esl(1), a=x1d:
                       e.activation(o, a, AF.Square),
                       waits=[tv2, tacc,
                              y_free[1] if (t == 0 and dc == 0) else None],
                       inc=sC)
            tacc = p.op("vector", lambda e, o=SQ2N, a=Esl(1):
                        e.tensor_add(o, o, a), waits=[t2s], inc=sV)
            x1_ready[dc] = (tv2, tacc)
        trs = p.op("tensor", lambda e, o=PB[5].ap()[0:1, :], l=ones_m1.ap(),
                   r=S2N: e.matmul(o, l, r, start=True, stop=True),
                   waits=[tacc, st2_free], inc=sT)
        tm = p.op("vector", lambda e, o=row(0), a=PB[5].ap()[0:1, :]:
                  e.tensor_scalar_mul(o, a, 1.0 / 2048.0),
                  waits=[trs], inc=sV)
        trq = p.op("tensor", lambda e, o=PB[5].ap()[0:1, :], l=ones_m1.ap(),
                   r=SQ2N: e.matmul(o, l, r, start=True, stop=True),
                   waits=[tm], inc=sT)
        te2 = p.op("vector", lambda e, o=row(1), a=PB[5].ap()[0:1, :]:
                   e.tensor_scalar_mul(o, a, 1.0 / 2048.0),
                   waits=[trq], inc=sV)
        st2_free = te2
        tm2 = p.op("scalar", lambda e, o=row(2), a=row(0):
                   e.activation(o, a, AF.Square), waits=[tm], inc=sC)
        tvv = p.op("vector", lambda e, o=row(1), a=row(1), b=row(2):
                   e.tensor_sub(o, a, b), waits=[tm2], inc=sV)
        tl = p.op("scalar", lambda e, o=row(2), a=row(1), b=c_eps_ln.ap():
                  e.activation(o, a, AF.Ln, bias=b), waits=[tvv], inc=sC)
        tA = p.op("scalar", lambda e, o=row(3), a=row(2), s2=c_nhalf.ap():
                  e.activation(o, a, AF.Exp, scale=s2), inc=sC)
        tB1 = p.op("vector", lambda e, o=row(2), a=row(0), b=row(3):
                   e.tensor_mul(o, a, b), waits=[tA, tl], inc=sV)
        p.op("vector", lambda e, o=row(2), a=row(2):
             e.tensor_scalar_mul(o, a, -1.0), inc=sV)
        tsl = p.op("vector", lambda e, o=kv_slk: e.tensor_copy(o, o), inc=sV)
        tba = p.op("tensor", lambda e, o=PB[4].ap(), l=ones_k1.ap(), r=row(3):
                   e.matmul(o, l, r, start=True, stop=True),
                   waits=[tA, bc_free, tsl], inc=sT)
        tca = p.op("vector", lambda e, o=Esl(2), a=PB[4].ap():
                   e.tensor_copy(o, a),
                   waits=[tba, qres["lastB"] if t == 0 else None], inc=sV)
        tbb = p.op("tensor", lambda e, o=PB[4].ap(), l=ones_k1.ap(), r=row(2):
                   e.matmul(o, l, r, start=True, stop=True),
                   waits=[tca, tsl], inc=sT)
        tcb = p.op("vector", lambda e, o=Esl(9), a=PB[4].ap():
                   e.tensor_copy(o, a),
                   waits=[tbb, mx(sst["abs"][0], sst["abs"][1])
                          if t == 0 else None,
                          qres["lastV"] if t == 0 else None], inc=sV)
        bc_free = tcb
        for dc in range(DC):
            x1d = Bfap[:, dc * 512:(dc + 1) * 512]
            tv2, tstat = x1_ready[dc]
            to1 = p.op("vector", lambda e, o=Esl(0), a=x1d, b=Esl(2):
                       e.tensor_mul(o, a, b), waits=[tca, tstat], inc=sV)
            to2 = p.op("vector", lambda e, o=x1d, a=Esl(0), b=Esl(9):
                       e.tensor_add(o, a, b), waits=[tcb], inc=sV)
            wr = dma("gpsimd",
                     out[dc * 128:(dc + 1) * 128, t * 512:(t + 1) * 512],
                     x1d, waits=[to2])
            x1_free[dc] = wr
            out_last = wr
            x1_ready[dc] = None
            if t + 1 < NT:
                emit_ln1_dc(t + 1, dc)

    p.op("sync", None, waits=[tok for tok in _q["gpsimd"]["last"]
                              if tok is not None])

    p.replay()
    es.close()
    return nc


_NC_CACHE = {}


def kernel(L0_attn_hidden_states, L0_attn_Wq, L0_attn_bq, L0_attn_Wk,
           L0_attn_bk, L0_attn_Wv, L0_attn_bv, L0_attn_Wo,
           L0_ffn_w1, L0_ffn_b1, L0_ffn_w2, L0_ffn_b2):
    x = np.ascontiguousarray(np.asarray(L0_attn_hidden_states, dtype=np.float32))
    Wq = np.asarray(L0_attn_Wq, dtype=np.float32)
    Wk = np.asarray(L0_attn_Wk, dtype=np.float32)
    Wv = np.asarray(L0_attn_Wv, dtype=np.float32)
    Wo = np.asarray(L0_attn_Wo, dtype=np.float32)
    w1 = np.asarray(L0_ffn_w1, dtype=np.float32)
    w2 = np.asarray(L0_ffn_w2, dtype=np.float32)

    wq_t = np.ascontiguousarray(
        Wq.reshape(16, 128, 16, 128).transpose(0, 3, 2, 1)).reshape(16, 128, 2048)
    wo_t = np.ascontiguousarray(
        Wo.reshape(16, 128, 16, 128).transpose(0, 3, 2, 1)).reshape(16, 128, 2048)
    wk_t = np.ascontiguousarray(Wk.T)
    wv_t = np.ascontiguousarray(Wv.T)
    w1_t = np.ascontiguousarray(
        w1.reshape(64, 128, 16, 128).transpose(0, 3, 2, 1)).reshape(64, 128, 2048)
    w2_t = np.ascontiguousarray(
        w2.reshape(16, 128, 64, 128).transpose(0, 3, 2, 1)).reshape(16, 128, 8192)
    bq_t = np.ascontiguousarray(
        np.asarray(L0_attn_bq, dtype=np.float32).reshape(16, 128).T)
    b1_t = np.ascontiguousarray(
        np.asarray(L0_ffn_b1, dtype=np.float32).reshape(64, 128).T)
    b2_t = np.ascontiguousarray(
        np.asarray(L0_ffn_b2, dtype=np.float32).reshape(16, 128).T)
    bk_t = np.ascontiguousarray(np.asarray(L0_attn_bk, dtype=np.float32)[None, :])
    bv_t = np.ascontiguousarray(np.asarray(L0_attn_bv, dtype=np.float32)[None, :])

    in_maps = []
    for c in range(8):
        b, half = c // 2, c % 2
        xtb = x[b].T
        own = xtb[:, half * 2048:(half + 1) * 2048]
        other = xtb[:, (1 - half) * 2048:(2 - half) * 2048]
        xt_c = np.ascontiguousarray(np.concatenate([own, other], axis=1))
        in_maps.append({
            "xt": xt_c, "wq": wq_t, "wk": wk_t, "wv": wv_t, "wo": wo_t,
            "w1": w1_t, "w2": w2_t, "bq": bq_t, "bk": bk_t, "bv": bv_t,
            "b1": b1_t, "b2": b2_t,
        })

    key = "dbg" if _DEBUG else "nc"
    if key not in _NC_CACHE:
        _NC_CACHE[key] = build_nc()
    res = run_bass_kernel_spmd(_NC_CACHE[key], in_maps, core_ids=list(range(8)))

    outp = np.empty((4, 4096, 2048), dtype=np.float32)
    for c in range(8):
        b, half = c // 2, c % 2
        outp[b, half * 2048:(half + 1) * 2048, :] = res.results[c]["out"].T
    if _DEBUG:
        return outp, res.results
    return outp


# version 10
# speedup vs baseline: 1.0751x; 1.0751x over previous
import contextlib
import numpy as np

import concourse.bass as bass
import concourse.mybir as mybir
from concourse.bass_utils import run_bass_kernel_spmd

F32 = mybir.dt.float32
BF16 = mybir.dt.bfloat16
AF = mybir.ActivationFunctionType
ALU = mybir.AluOpType

D = 2048
DC = 16
TOWN = 2048
NT = 4
S = 4096
NSC = 32
NFC = 64
KVH = 4
EPS_ATTN = 1e-6
EPS_LN = 1e-5
NW = float(8192 * 2048)
CBAR = float(np.float32(0.7978845608))
K1024C = float(np.float32(0.7978845608)) * 1024.0
C24 = float(2.0 ** -24)
C07 = float(np.float32(0.7))
CD7 = float(0.7 - float(np.float32(0.7)))

ENGINES = ("sync", "gpsimd", "vector", "scalar", "tensor")


class Sem:
    __slots__ = ("name", "handle", "producer", "count", "is_dma")

    def __init__(self, name, handle, producer, is_dma):
        self.name, self.handle, self.producer = name, handle, producer
        self.count, self.is_dma = 0, is_dma


def mx(*toks):
    best = None
    for t in toks:
        if t is None:
            continue
        if best is None or (t[0] is best[0] and t[1] > best[1]):
            best = t
        elif t[0] is not best[0]:
            raise ValueError("mx() needs same-sem tokens")
    return best


class Prog:

    AUTO_CHAIN = ("vector", "scalar", "gpsimd")

    def __init__(self, nc, es):
        self.nc = nc
        self.es = es
        self.ops = {e: [] for e in ENGINES}
        self.watermark = {e: {} for e in ENGINES}

    def sem(self, name, producer, is_dma=False):
        h = self.es.enter_context(self.nc.semaphore(name))
        return Sem(name, h, producer, is_dma)

    def op(self, eng, fn, waits=(), inc=None, inc_amt=1):
        waits = list(waits)
        if eng in self.AUTO_CHAIN and inc is not None and inc.count > 0:
            waits.append((inc, inc.count))
        wl = []
        for tok in waits:
            if tok is None:
                continue
            s, v = tok
            if v is None or v <= 0:
                continue
            assert v <= s.count, (
                f"deadlock: {eng} waits {s.name}>={v} but only {s.count} emitted")
            if self.watermark[eng].get(s.name, 0) >= v:
                continue
            self.watermark[eng][s.name] = v
            wl.append((s.handle, v))
        tok = None
        if inc is not None:
            assert inc.producer == eng, (eng, inc.name, inc.producer)
            inc.count += inc_amt
            tok = (inc, inc.count)
        self.ops[eng].append((fn, wl, (inc.handle, inc_amt) if inc else None))
        return tok

    def replay(self):
        with self.nc.Block() as block:
            def mk(ename):
                def go(eng):
                    for fn, waits, inc in self.ops[ename]:
                        for h, v in waits:
                            eng.wait_ge(h, v)
                        ins = eng.nop() if fn is None else fn(eng)
                        if inc is not None:
                            ins.then_inc(*inc)
                return go
            block.sync(mk("sync"))
            block.gpsimd(mk("gpsimd"))
            block.vector(mk("vector"))
            block.scalar(mk("scalar"))
            block.tensor(mk("tensor"))


_DEBUG = False


def build_nc():
    nc = bass.Bass("TRN2", target_bir_lowering=False, debug=False)
    es = contextlib.ExitStack()
    _dbg_kind = "ExternalOutput" if _DEBUG else "Internal"

    xt = nc.dram_tensor("xt", [D, S], F32, kind="ExternalInput").ap()
    wq = nc.dram_tensor("wq", [16, 128, 2048], F32, kind="ExternalInput").ap()
    wk = nc.dram_tensor("wk", [D, 512], F32, kind="ExternalInput").ap()
    wv = nc.dram_tensor("wv", [D, 512], F32, kind="ExternalInput").ap()
    wo = nc.dram_tensor("wo", [16, 128, 2048], F32, kind="ExternalInput").ap()
    w1 = nc.dram_tensor("w1", [64, 128, 2048], F32, kind="ExternalInput").ap()
    w2 = nc.dram_tensor("w2", [16, 128, 8192], F32, kind="ExternalInput").ap()
    bq = nc.dram_tensor("bq", [128, 16], F32, kind="ExternalInput").ap()
    bk = nc.dram_tensor("bk", [1, 512], F32, kind="ExternalInput").ap()
    bv = nc.dram_tensor("bv", [1, 512], F32, kind="ExternalInput").ap()
    b1 = nc.dram_tensor("b1", [128, 64], F32, kind="ExternalInput").ap()
    b2 = nc.dram_tensor("b2", [128, 16], F32, kind="ExternalInput").ap()
    out = nc.dram_tensor("out", [D, TOWN], F32, kind="ExternalOutput").ap()
    w1q = nc.dram_tensor("w1q", [64, 128, 2048], BF16, kind=_dbg_kind).ap()
    w2q = nc.dram_tensor("w2q", [16, 128, 8192], BF16, kind=_dbg_kind).ap()
    y1s = nc.dram_tensor("y1s", [NT, DC, 128, 512], F32, kind=_dbg_kind).ap()
    if _DEBUG:
        dbg_kv = nc.dram_tensor("dbg_kv", [128, 4 * 129], BF16,
                                kind="ExternalOutput").ap()
        dbg_thr = nc.dram_tensor("dbg_thr", [128, 4], F32,
                                 kind="ExternalOutput").ap()

    A = nc.alloc_sbuf_tensor("arnA", [128, 32768], BF16)
    B = nc.alloc_sbuf_tensor("arnB", [128, 32768], BF16)
    Bf = B.bitcast(F32)
    C = nc.alloc_sbuf_tensor("arnC", [128, 16384], BF16)
    Cf = C.bitcast(F32)
    E = nc.alloc_sbuf_tensor("arnE", [128, 10 * 512], F32)
    Eb = E.bitcast(BF16)
    Fst = nc.alloc_sbuf_tensor("arnF", [128, 4 * 512], F32)
    Gsq = nc.alloc_sbuf_tensor("arnG", [128, 4096], BF16)
    rows = nc.alloc_sbuf_tensor("rows", [1, 4 * 512], F32)
    kvsb = nc.alloc_sbuf_tensor("kvsb", [128, 4 * 129], BF16)
    bq_sb = nc.alloc_sbuf_tensor("bq_sb", [128, 16], F32)
    b1_sb = nc.alloc_sbuf_tensor("b1_sb", [128, 64], F32)
    b2_sb = nc.alloc_sbuf_tensor("b2_sb", [128, 16], F32)
    thr = nc.alloc_sbuf_tensor("thr", [128, 4], F32)
    acc1 = nc.alloc_sbuf_tensor("acc1", [128, 1], F32)
    acc2 = nc.alloc_sbuf_tensor("acc2", [128, 1], F32)
    acc_t = nc.alloc_sbuf_tensor("acc_t", [128, 2], F32)
    ones_m1 = nc.alloc_sbuf_tensor("ones_m1", [128, 1], F32)
    ones_k1 = nc.alloc_sbuf_tensor("ones_k1", [1, 128], F32)
    c_nhalf = nc.alloc_sbuf_tensor("c_nhalf", [1, 1], F32)
    c_none = nc.alloc_sbuf_tensor("c_none", [1, 1], F32)
    c_eps_at = nc.alloc_sbuf_tensor("c_eps_at", [1, 1], F32)
    c_eps_ln = nc.alloc_sbuf_tensor("c_eps_ln", [1, 1], F32)
    tot_sb = nc.alloc_sbuf_tensor("tot_sb", [1, 4], F32)
    c_mu = nc.alloc_sbuf_tensor("c_mu", [1, 1], F32)
    PB = [nc.alloc_psum_tensor(f"PB{i}", [128, 512], F32) for i in range(8)]

    Gf = Gsq.bitcast(F32)
    Aap, Bbf, Cap, Gap = A.ap(), B.ap(), C.ap(), Gsq.ap()
    Bfap, Cfap, Gfap = Bf.ap(), Cf.ap(), Gf.ap()

    def Esl(i, n=1):
        return E.ap()[:, i * 512:(i + n) * 512]

    def row(i):
        return rows.ap()[0:1, i * 512:(i + 1) * 512]

    p = Prog(nc, es)
    NLANE = 6
    sSl = [p.sem(f"sS{i}", "sync", is_dma=True) for i in range(NLANE)]
    sGl = [p.sem(f"sG{i}", "gpsimd", is_dma=True) for i in range(NLANE)]
    sT = p.sem("sT", "tensor")
    sP = p.sem("sP", "gpsimd")
    sV = p.sem("sV", "vector")
    sC = p.sem("sC", "scalar")
    _q = {"sync": {"rr": 0, "lanes": sSl, "last": [None] * NLANE},
          "gpsimd": {"rr": 0, "lanes": sGl, "last": [None] * NLANE}}

    def dma(qeng, dst, src, waits=()):
        q = _q[qeng]
        i = q["rr"]
        q["rr"] = (i + 1) % NLANE
        tok = p.op(qeng, lambda e, a=dst, b=src: e.dma_start(out=a, in_=b),
                   waits=list(waits) + [q["last"][i]],
                   inc=q["lanes"][i], inc_amt=16)
        q["last"][i] = tok
        return tok

    st_ = {}
    for nm, t_, val in (("ones_m1", ones_m1, 1.0), ("ones_k1", ones_k1, 1.0),
                        ("c_nhalf", c_nhalf, -0.5), ("c_none", c_none, -1.0),
                        ("c_eps_at", c_eps_at, EPS_ATTN),
                        ("c_eps_ln", c_eps_ln, EPS_LN),
                        ("acc1", acc1, 0.0), ("acc2", acc2, 0.0)):
        st_[nm] = p.op("vector", lambda e, a=t_.ap(), v=val: e.memset(a, v), inc=sV)
    st_["F0"] = p.op("vector", lambda e, a=Fst.ap(): e.memset(a, 0.0), inc=sV)
    p.op("vector", lambda e, a=c_mu.ap(): e.memset(a, 1.0), inc=sV)
    p.op("vector", lambda e, a=c_mu.ap():
         e.tensor_scalar_add(a, a, float(2.0 ** -29)), inc=sV)
    p.op("vector", lambda e, a=c_mu.ap():
         e.tensor_scalar_add(a, a, -1.0), inc=sV)
    st_["c_mu"] = p.op("vector", lambda e, a=c_mu.ap():
                       e.tensor_scalar_mul(a, a, float(2.0 ** 29) * CD7),
                       inc=sV)
    t_ones_sv = None
    for par in range(2):
        for h in range(KVH):
            c0 = 1024 + par * 520 + h * 129 + 128
            t_ones_sv = p.op("vector", lambda e, a=Gap[:, c0:c0 + 1]:
                             e.memset(a, 1.0), inc=sV)
    t_bq = dma("sync", bq_sb.ap(), bq)
    t_b1 = dma("sync", b1_sb.ap(), b1)
    t_b2 = dma("sync", b2_sb.ap(), b2)
    t_bkr = dma("sync", row(0), bk)
    t_bvr = dma("sync", row(1), bv)
    t_bkmm = p.op("tensor", lambda e: e.matmul(PB[6].ap(), ones_k1.ap(), row(0),
                                               start=True, stop=True),
                  waits=[t_bkr, st_["ones_k1"]], inc=sT)
    t_bvmm = p.op("tensor", lambda e: e.matmul(PB[7].ap(), ones_k1.ap(), row(1),
                                               start=True, stop=True),
                  waits=[t_bvr], inc=sT)
    bkbc, bvbc = Esl(3), Esl(4)
    t_bkbc = p.op("vector", lambda e: e.tensor_copy(bkbc, PB[6].ap()),
                  waits=[t_bkmm], inc=sV)
    t_bvbc = p.op("vector", lambda e: e.tensor_copy(bvbc, PB[7].ap()),
                  waits=[t_bvmm], inc=sV)

    t_xown, t_xpart = [], []
    for dc in range(DC):
        t_xown.append(dma("gpsimd", Aap[:, dc * 2048:(dc + 1) * 2048],
                          xt[dc * 128:(dc + 1) * 128, 0:TOWN]))
    wkT = Cap[:, 0:8192].rearrange("p (dc e) -> p dc e", e=512)
    wvT = Cap[:, 8192:16384].rearrange("p (dc e) -> p dc e", e=512)
    t_wk = dma("gpsimd", wkT, wk.rearrange("(dc p) e -> p dc e", p=128))
    t_wv = dma("gpsimd", wvT, wv.rearrange("(dc p) e -> p dc e", p=128))
    for dc in range(DC):
        t_xpart.append(dma("gpsimd", Bbf[:, dc * 2048:(dc + 1) * 2048],
                           xt[dc * 128:(dc + 1) * 128, TOWN:S]))

    scale_jobs = [("w1", w1[fc, :, h2 * 1024:(h2 + 1) * 1024])
                  for fc in range(64) for h2 in range(2)]
    scale_jobs += [("w2", w2[dcq, :, e8 * 1024:(e8 + 1) * 1024])
                   for dcq in range(16) for e8 in range(8)]
    Eb9 = Eb.ap()[:, 9 * 1024:10 * 1024]
    sst = {"i": 0, "abs": [None, None],
           "acc": {"w1": st_["acc1"], "w2": st_["acc2"]}, "done": False}

    def emit_scale(njobs):
        for _ in range(njobs):
            i = sst["i"]
            if i >= len(scale_jobs):
                break
            sst["i"] += 1
            which, src = scale_jobs[i]
            sl = i % 2
            stage = Esl(5 + 2 * sl, 2)
            acol = acc_t.ap()[:, sl:sl + 1]
            tok_d = dma("sync", stage, src, waits=[sst["abs"][sl]])
            tok_a = p.op("scalar",
                         lambda e, s2=stage, a2=acol:
                         e.activation(s2, s2, AF.Abs, accum_out=a2),
                         waits=[tok_d, sst["acc"]["w1"], sst["acc"]["w2"]],
                         inc=sC)
            sst["abs"][sl] = tok_a
            accd = acc1.ap() if which == "w1" else acc2.ap()
            sst["acc"][which] = p.op(
                "vector", lambda e, a2=accd, c2=acol:
                e.scalar_tensor_tensor(a2, c2, K1024C, a2, ALU.subtract, ALU.add),
                waits=[tok_a], inc=sV)
        if sst["i"] >= len(scale_jobs):
            sst["done"] = True

    kfree = [None, None]
    vfree = [None, None]
    skvfree = [None, None]
    pend_kv = None
    for sc in range(NSC):
        par = sc % 2
        own = sc < 16
        base = (sc if own else sc - 16) * 128
        srcA = Aap if own else Bbf
        xw = (t_xown if own else t_xpart)
        xtiles = [srcA[:, dc * 2048 + base: dc * 2048 + base + 128]
                  for dc in range(DC)]
        tk = tv = None
        for dc in range(DC):
            tk = p.op("tensor",
                      lambda e, o=PB[par].ap(), l=xtiles[dc], r=wkT[:, dc, :],
                      s=(dc == 0), z=(dc == DC - 1):
                      e.matmul(o, l, r, start=s, stop=z),
                      waits=[xw[dc], t_wk, kfree[par]],
                      inc=sT if dc == DC - 1 else None)
        if pend_kv is not None:
            pend_kv()
            pend_kv = None
        for dc in range(DC):
            tv = p.op("tensor",
                      lambda e, o=PB[2 + par].ap(), l=xtiles[dc], r=wvT[:, dc, :],
                      s=(dc == 0), z=(dc == DC - 1):
                      e.matmul(o, l, r, start=s, stop=z),
                      waits=[t_wv, vfree[par]],
                      inc=sT if dc == DC - 1 else None)
        sk_t = Gap[:, par * 512:(par + 1) * 512]
        svb = 1024 + par * 520
        t1 = p.op("vector", lambda e, o=Esl(0), a=PB[par].ap(), b=bkbc:
                  e.tensor_add(o, a, b), waits=[tk, t_bkbc], inc=sV)
        t2 = p.op("vector", lambda e, o=Esl(1), a=Esl(0):
                  e.tensor_scalar_min(o, a, 0.0), inc=sV)
        t3 = p.op("vector", lambda e, o=Esl(2), a=Esl(0):
                  e.tensor_scalar_max(o, a, 0.0), inc=sV)
        kfree[par] = t3
        t4 = p.op("scalar", lambda e, o=Esl(1), a=Esl(1):
                  e.activation(o, a, AF.Exp), waits=[t2], inc=sC)
        t5 = p.op("vector", lambda e, o=sk_t, a=Esl(1), b=Esl(2):
                  e.tensor_add(o, a, b), waits=[t4, skvfree[par]], inc=sV)
        t6 = None
        for h in range(KVH):
            t6 = p.op("vector",
                      lambda e, o=Gap[:, svb + h * 129: svb + h * 129 + 128],
                      a=PB[2 + par].ap()[:, h * 128:(h + 1) * 128],
                      b=bvbc[:, h * 128:(h + 1) * 128]:
                      e.scalar_tensor_tensor(o, a, 1.0, b, ALU.mult, ALU.add),
                      waits=[tv, t_bvbc], inc=sV)
        vfree[par] = t6

        def mk_kv(sc=sc, par=par, sk_t=sk_t, svb=svb, t5=t5, t6=t6):
            tkv = None
            for h in range(KVH):
                tkv = p.op("tensor",
                           lambda e, o=PB[4 + h].ap()[:, 0:129],
                           l=sk_t[:, h * 128:(h + 1) * 128],
                           r=Gap[:, svb + h * 129: svb + h * 129 + 129],
                           s=(sc == 0), z=(sc == NSC - 1):
                           e.matmul(o, l, r, start=s, stop=z),
                           waits=[t5, t6, t_ones_sv],
                           inc=sT if h == KVH - 1 else None)
            skvfree[par] = tkv
            return tkv
        pend_kv = mk_kv
        emit_scale(6)
    last_kv_mm = pend_kv()

    t_kv_ev = None
    for h in range(KVH):
        t_kv_ev = p.op("vector",
                       lambda e, o=kvsb.ap()[:, h * 129:(h + 1) * 129],
                       a=PB[4 + h].ap()[:, 0:129]: e.tensor_copy(o, a),
                       waits=[last_kv_mm], inc=sV)
    if _DEBUG:
        dma("gpsimd", dbg_kv, kvsb.ap(), waits=[t_kv_ev])

    def wq_buf(i):
        return Cap[:, (i % 2) * 2048:((i % 2) + 1) * 2048]

    def wo_buf(i):
        return Cap[:, 4096 + (i % 2) * 2048: 4096 + ((i % 2) + 1) * 2048]

    wq_tok = [None] * 16
    for i in range(2):
        wq_tok[i] = dma("gpsimd", wq_buf(i), wq[i], waits=[last_kv_mm])

    def sq_tile(ic, t):
        return Gap[:, (ic % 2) * 2048 + t * 512: (ic % 2) * 2048 + (t + 1) * 512]

    q_free = [None, None]
    qkv_free = nrm_free = rbc_free = None
    attnT_tok = [None] * 16
    qkv_last_mm = [None] * 16
    sq_toks = [[None] * NT for _ in range(16)]
    uro = [None, None]
    last_q_mm = [None] * 16
    thr_done = None

    def emit_qkv(ic, t):
        nonlocal qkv_free, nrm_free, rbc_free
        kvh = ic // 4
        kv_l = kvsb.ap()[:, kvh * 129: kvh * 129 + 128]
        ks_l = kvsb.ap()[:, kvh * 129 + 128: kvh * 129 + 129]
        sq_t = sq_tile(ic, t)
        tqkv = p.op("tensor", lambda e, o=PB[2].ap(), l=kv_l, r=sq_t:
                    e.matmul(o, l, r, start=True, stop=True),
                    waits=[sq_toks[ic][t], t_kv_ev, qkv_free], inc=sT)
        tnrm = p.op("tensor", lambda e, o=PB[3].ap()[0:1, :], l=ks_l, r=sq_t:
                    e.matmul(o, l, r, start=True, stop=True),
                    waits=[nrm_free], inc=sT)
        qkv_last_mm[ic] = tnrm
        j = (ic * NT + t) % 2
        u_r, r_r = row(0 + j), row(2 + j)
        tu = p.op("scalar", lambda e, o=u_r, a=PB[3].ap()[0:1, :],
                  b=c_eps_at.ap():
                  e.activation(o, a, AF.Ln, bias=b),
                  waits=[tnrm, uro[j], st_["c_eps_at"],
                         t_bkr if ic == 0 else None,
                         t_bvr if ic == 0 else None], inc=sC)
        nrm_free = tu
        tr = p.op("scalar", lambda e, o=r_r, a=u_r, s2=c_none.ap():
                  e.activation(o, a, AF.Exp, scale=s2),
                  waits=[st_["c_none"]], inc=sC)
        tbc = p.op("tensor", lambda e, o=PB[4].ap(), l=ones_k1.ap(), r=r_r:
                   e.matmul(o, l, r, start=True, stop=True),
                   waits=[tr, rbc_free], inc=sT)
        uro[j] = tbc
        tcp = p.op("vector", lambda e, o=Esl(0), a=PB[4].ap():
                   e.tensor_copy(o, a), waits=[tbc], inc=sV)
        rbc_free = tcp
        tdv = p.op("vector",
                   lambda e,
                   o=Bbf[:, ic * 2048 + t * 512: ic * 2048 + (t + 1) * 512],
                   a=PB[2].ap(), b=Esl(0):
                   e.tensor_mul(o, a, b), waits=[tqkv], inc=sV)
        qkv_free = tdv
        if t == NT - 1:
            attnT_tok[ic] = tdv

    for ic in range(16):
        for t in range(NT):
            par = (ic * NT + t) % 2
            tq = None
            for dc in range(DC):
                tq = p.op("tensor",
                          lambda e, o=PB[par].ap(),
                          l=wq_buf(ic)[:, dc * 128:(dc + 1) * 128],
                          r=Aap[:, dc * 2048 + t * 512: dc * 2048 + (t + 1) * 512],
                          s=(dc == 0), z=(dc == DC - 1):
                          e.matmul(o, l, r, start=s, stop=z),
                          waits=[wq_tok[ic], q_free[par],
                                 qkv_last_mm[ic - 2] if (t == 0 and ic >= 2)
                                 else None],
                          inc=sT if dc == DC - 1 else None)
            bqc = bq_sb.ap()[:, ic:ic + 1]
            m1_ = p.op("vector", lambda e, o=Esl(1), a=PB[par].ap(), b=bqc:
                       e.tensor_scalar(o, a, b, 0.0, ALU.add, ALU.min),
                       waits=[tq, t_bq], inc=sV)
            m2_ = p.op("vector", lambda e, o=Esl(2), a=PB[par].ap(), b=bqc:
                       e.tensor_scalar(o, a, b, 0.0, ALU.add, ALU.max),
                       inc=sV)
            q_free[par] = m2_
            m3_ = p.op("scalar", lambda e, o=Esl(1), a=Esl(1):
                       e.activation(o, a, AF.Exp), waits=[m1_], inc=sC)
            m4_ = p.op("vector", lambda e, o=sq_tile(ic, t), a=Esl(1), b=Esl(2):
                       e.tensor_add(o, a, b), waits=[m3_], inc=sV)
            sq_toks[ic][t] = m4_
            if ic >= 1:
                emit_qkv(ic - 1, t)
        last_q_mm[ic] = tq
        if ic + 2 < 16:
            wq_tok[ic + 2] = dma("gpsimd", wq_buf(ic + 2), wq[ic + 2],
                                 waits=[tq])
        emit_scale(8)
        if sst["done"] and thr_done is None:
            slk0 = p.op("vector", lambda e, o=Fst.ap()[:, 0:512]:
                        e.tensor_copy(o, o),
                        waits=[sst["acc"]["w1"], sst["acc"]["w2"], st_["F0"]],
                        inc=sV)
            tw1 = p.op("tensor", lambda e, o=PB[7].ap()[0:1, 0:1],
                       l=ones_m1.ap(), r=acc1.ap():
                       e.matmul(o, l, r, start=True, stop=True),
                       waits=[slk0, t_kv_ev, st_["ones_m1"]], inc=sT)
            tw2 = p.op("tensor", lambda e, o=PB[7].ap()[0:1, 4:5],
                       l=ones_m1.ap(), r=acc2.ap():
                       e.matmul(o, l, r, start=True, stop=True), inc=sT)
            p.op("vector", lambda e, o=tot_sb.ap()[0:1, 0:1],
                 a=PB[7].ap()[0:1, 0:1]:
                 e.tensor_scalar(o, a, C24, CBAR, ALU.mult, ALU.add),
                 waits=[tw1, tw2], inc=sV)
            p.op("vector", lambda e, o=tot_sb.ap()[0:1, 1:2],
                 a=PB[7].ap()[0:1, 4:5]:
                 e.tensor_scalar(o, a, C24, CBAR, ALU.mult, ALU.add), inc=sV)
            p.op("vector", lambda e, o=Fst.ap()[:, 0:512]: e.tensor_copy(o, o),
                 inc=sV)
            p.op("vector", lambda e, o=acc_t.ap()[0:1, 0:1],
                 a=tot_sb.ap()[0:1, 0:1]: e.tensor_scalar_mul(o, a, C07),
                 inc=sV)
            p.op("vector", lambda e, o=acc_t.ap()[0:1, 1:2],
                 a=tot_sb.ap()[0:1, 1:2]: e.tensor_scalar_mul(o, a, C07),
                 inc=sV)
            p.op("vector", lambda e, o=tot_sb.ap()[0:1, 2:3],
                 a=tot_sb.ap()[0:1, 0:1], b=c_mu.ap()[0:1, 0:1]:
                 e.tensor_mul(o, a, b), inc=sV)
            p.op("vector", lambda e, o=tot_sb.ap()[0:1, 3:4],
                 a=tot_sb.ap()[0:1, 1:2], b=c_mu.ap()[0:1, 0:1]:
                 e.tensor_mul(o, a, b), inc=sV)
            p.op("vector", lambda e, o=Fst.ap()[:, 0:512]: e.tensor_copy(o, o),
                 inc=sV)
            p.op("vector", lambda e, o=tot_sb.ap()[0:1, 1:2],
                 a=acc_t.ap()[0:1, 0:1], b=tot_sb.ap()[0:1, 2:3]:
                 e.tensor_add(o, a, b), inc=sV)
            p.op("vector", lambda e, o=tot_sb.ap()[0:1, 3:4],
                 a=acc_t.ap()[0:1, 1:2], b=tot_sb.ap()[0:1, 3:4]:
                 e.tensor_add(o, a, b), inc=sV)
            p.op("vector", lambda e, o=Fst.ap()[:, 0:512]: e.tensor_copy(o, o),
                 inc=sV)
            p.op("vector", lambda e, o=tot_sb.ap()[0:1, 0:1],
                 a=tot_sb.ap()[0:1, 1:2]: e.tensor_scalar_mul(o, a, -1.0),
                 inc=sV)
            p.op("vector", lambda e, o=tot_sb.ap()[0:1, 2:3],
                 a=tot_sb.ap()[0:1, 3:4]: e.tensor_scalar_mul(o, a, -1.0),
                 inc=sV)
            p.op("vector", lambda e, o=Fst.ap()[:, 0:512]: e.tensor_copy(o, o),
                 inc=sV)
            slk1 = p.op("vector", lambda e, o=Fst.ap()[:, 0:512]:
                        e.tensor_copy(o, o), inc=sV)
            tbc2 = p.op("tensor", lambda e, o=PB[7].ap()[:, 8:12],
                        l=ones_k1.ap(), r=tot_sb.ap()[0:1, 0:4]:
                        e.matmul(o, l, r, start=True, stop=True),
                        waits=[slk1], inc=sT)
            thr_done = p.op("vector", lambda e, o=thr.ap(),
                            a=PB[7].ap()[:, 8:12]:
                            e.tensor_copy(o, a), waits=[tbc2], inc=sV)
            if _DEBUG:
                dma("gpsimd", dbg_thr, thr.ap(), waits=[thr_done])
    for t in range(NT):
        emit_qkv(15, t)
    assert thr_done is not None

    def Ssl(t):
        return Fst.ap()[:, (2 * t) * 512:(2 * t + 1) * 512] if t < 2 \
            else Esl(5 + 2 * (t - 2))

    def Sqsl(t):
        return Fst.ap()[:, (2 * t + 1) * 512:(2 * t + 2) * 512] if t < 2 \
            else Esl(6 + 2 * (t - 2))

    stz = [None] * NT
    for t in range(NT):
        w8 = [st_["F0"]]
        if t >= 2:
            w8 = [mx(sst["abs"][0], sst["abs"][1]), thr_done]
        p.op("vector", lambda e, o=Ssl(t): e.memset(o, 0.0), waits=w8, inc=sV)
        stz[t] = p.op("vector", lambda e, o=Sqsl(t): e.memset(o, 0.0), inc=sV)

    wo_tok = [None] * 16
    for i in range(2):
        wo_tok[i] = dma("gpsimd", wo_buf(i), wo[i], waits=[last_kv_mm])
    wo_free = [None, None]
    xres_tok = [None, None]
    xres_free = [None, None]
    y_free = [None, None]
    y1_wr = [[None] * DC for _ in range(NT)]
    wo_mm_done = None
    sqprev = None
    for dc in range(DC):
        for t in range(NT):
            par = (dc * NT + t) % 2
            two = None
            for icc in range(16):
                two = p.op("tensor",
                           lambda e, o=PB[5 + par].ap(),
                           l=wo_buf(dc)[:, icc * 128:(icc + 1) * 128],
                           r=Bbf[:, icc * 2048 + t * 512: icc * 2048 + (t + 1) * 512],
                           s=(icc == 0), z=(icc == 15):
                           e.matmul(o, l, r, start=s, stop=z),
                           waits=[wo_tok[dc], attnT_tok[icc], wo_free[par],
                                  thr_done if (dc == 0 and t == 0) else None],
                           inc=sT if icc == 15 else None)
            wo_mm_done = two
            xs = (dc * NT + t) % 2
            xres = Esl(3 + xs)
            xres_tok[xs] = dma("gpsimd", xres,
                               xt[dc * 128:(dc + 1) * 128, t * 512:(t + 1) * 512],
                               waits=[xres_free[xs],
                                      t_bkbc if (dc == 0 and t < 2) else None,
                                      t_bvbc if (dc == 0 and t < 2) else None])
            ys = (dc * NT + t) % 2
            ybuf = Esl(0 + ys)
            ty = p.op("vector", lambda e, o=ybuf, a=PB[5 + par].ap(), b=xres:
                      e.tensor_add(o, a, b),
                      waits=[two, xres_tok[xs], y_free[ys]], inc=sV)
            xres_free[xs] = ty
            wo_free[par] = ty
            y1_wr[t][dc] = dma("gpsimd", y1s[t, dc], ybuf, waits=[ty])
            y_free[ys] = y1_wr[t][dc]
            p.op("vector", lambda e, o=Ssl(t), a=ybuf:
                 e.tensor_add(o, o, a), waits=[stz[t]], inc=sV)
            tsq = p.op("scalar", lambda e, o=Esl(2), a=ybuf:
                       e.activation(o, a, AF.Square),
                       waits=[ty, sqprev], inc=sC)
            sqprev = p.op("vector", lambda e, o=Sqsl(t), a=Esl(2):
                          e.tensor_add(o, o, a), waits=[tsq], inc=sV)
        if dc + 2 < 16:
            wo_tok[dc + 2] = dma("gpsimd", wo_buf(dc + 2), wo[dc + 2],
                                 waits=[wo_mm_done])

    def Abc1(t):
        return Fst.ap()[:, (2 * t) * 512:(2 * t + 1) * 512] if t < 2 \
            else Gfap[:, (2 * (t - 2)) * 512:(2 * (t - 2) + 1) * 512]

    def Bbc1(t):
        return Fst.ap()[:, (2 * t + 1) * 512:(2 * t + 2) * 512] if t < 2 \
            else Gfap[:, (2 * (t - 2) + 1) * 512:(2 * (t - 2) + 2) * 512]

    ab_done = [None] * NT
    st_free = None
    kv_slk = kvsb.ap()
    for t in range(NT):
        trs = p.op("tensor", lambda e, o=PB[7].ap()[0:1, :], l=ones_m1.ap(),
                   r=Ssl(t): e.matmul(o, l, r, start=True, stop=True),
                   waits=[sqprev, st_free], inc=sT)
        tm = p.op("vector", lambda e, o=row(0), a=PB[7].ap()[0:1, :]:
                  e.tensor_scalar_mul(o, a, 1.0 / 2048.0),
                  waits=[trs], inc=sV)
        trq = p.op("tensor", lambda e, o=PB[7].ap()[0:1, :], l=ones_m1.ap(),
                   r=Sqsl(t): e.matmul(o, l, r, start=True, stop=True),
                   waits=[tm], inc=sT)
        te2 = p.op("vector", lambda e, o=row(1), a=PB[7].ap()[0:1, :]:
                   e.tensor_scalar_mul(o, a, 1.0 / 2048.0),
                   waits=[trq], inc=sV)
        tm2 = p.op("scalar", lambda e, o=row(2), a=row(0):
                   e.activation(o, a, AF.Square), waits=[tm], inc=sC)
        tvv = p.op("vector", lambda e, o=row(1), a=row(1), b=row(2):
                   e.tensor_sub(o, a, b), waits=[tm2], inc=sV)
        tl = p.op("scalar", lambda e, o=row(2), a=row(1), b=c_eps_ln.ap():
                  e.activation(o, a, AF.Ln, bias=b),
                  waits=[tvv, st_["c_eps_ln"]], inc=sC)
        tA = p.op("scalar", lambda e, o=row(3), a=row(2), s2=c_nhalf.ap():
                  e.activation(o, a, AF.Exp, scale=s2),
                  waits=[st_["c_nhalf"]], inc=sC)
        tB1 = p.op("vector", lambda e, o=row(2), a=row(0), b=row(3):
                   e.tensor_mul(o, a, b), waits=[tA, tl], inc=sV)
        p.op("vector", lambda e, o=row(2), a=row(2):
             e.tensor_scalar_mul(o, a, -1.0), inc=sV)
        tsl = p.op("vector", lambda e, o=kv_slk: e.tensor_copy(o, o),
                   waits=[qkv_last_mm[15]], inc=sV)
        tba = p.op("tensor", lambda e, o=PB[7].ap(), l=ones_k1.ap(), r=row(3):
                   e.matmul(o, l, r, start=True, stop=True),
                   waits=[tA, tsl], inc=sT)
        tca = p.op("vector", lambda e, o=Abc1(t), a=PB[7].ap():
                   e.tensor_copy(o, a), waits=[tba, sqprev], inc=sV)
        tbb = p.op("tensor", lambda e, o=PB[7].ap(), l=ones_k1.ap(), r=row(2):
                   e.matmul(o, l, r, start=True, stop=True),
                   waits=[tca, tsl], inc=sT)
        tcb = p.op("vector", lambda e, o=Bbc1(t), a=PB[7].ap():
                   e.tensor_copy(o, a), waits=[tbb], inc=sV)
        ab_done[t] = tcb
        st_free = tcb

    S2N, SQ2N = Esl(3), Esl(4)
    h_free = [None, None]
    out_free = [None, None]
    bc_free = None
    st2_free = None
    w1q_tok = [None, None]
    w1q_free = [None, None]
    w2q_tok = [None, None, None]
    w2q_free = [None, None, None]
    x1_ready = [None] * DC
    x1bf_ready = [None] * DC
    x1_free = [None] * DC
    hbf_tok = [None] * NFC
    out_last = None
    qres = {"rr": 0, "stage_free": [None] * 4, "b_free": [None, None],
            "wr_w1": [[None] * 4 for _ in range(64)],
            "wr_w2": [[None] * 16 for _ in range(16)], "lastB": None}
    gwr_w1 = [[], []]
    gwr_w2 = [[], [], []]
    stats_guard = ab_done[3]
    abs_guard = mx(sst["abs"][0], sst["abs"][1])

    def quant_piece(which, idx, pc512, dst, dram_dst, dst_waits):
        first = qres["rr"] == 0
        sl = qres["rr"] % 4
        bl = qres["rr"] % 2
        qres["rr"] += 1
        stage = Esl(5 + sl)
        btile = Esl(9) if bl == 0 else Esl(2)
        if which == "w1":
            srcw = w1[idx, :, pc512 * 512:(pc512 + 1) * 512]
            tpos, tneg = thr.ap()[:, 1:2], thr.ap()[:, 0:1]
        else:
            srcw = w2[idx, :, pc512 * 512:(pc512 + 1) * 512]
            tpos, tneg = thr.ap()[:, 3:4], thr.ap()[:, 2:3]
        tok_d = dma("sync", stage, srcw,
                    waits=[qres["stage_free"][sl],
                           stats_guard if first else None])
        tb = p.op("vector", lambda e, o=btile, a=stage, b2=tneg:
                  e.tensor_single_scalar(o, a, b2, ALU.is_le),
                  waits=[tok_d, thr_done, qres["b_free"][bl],
                         sqprev if first else None,
                         abs_guard if first else None], inc=sV)
        qres["lastB"] = tb
        tqv = p.op("vector", lambda e, o=dst, a=stage, s2=tpos, b2=btile:
                   e.scalar_tensor_tensor(o, a, s2, b2, ALU.is_ge, ALU.subtract),
                   waits=[tb] + list(dst_waits), inc=sV)
        qres["stage_free"][sl] = tqv
        qres["b_free"][bl] = tqv
        tw = dma("gpsimd", dram_dst, dst, waits=[tqv])
        return tqv, tw

    def w1q_buf(i):
        return Cap[:, (i % 2) * 2048:((i % 2) + 1) * 2048]

    def w2q_slot(i):
        return Cap[:, 4096 + (i % 3) * 4096: 4096 + ((i % 3) + 1) * 4096]

    for t in range(NT):
        for dc in range(DC):
            x1d = Bfap[:, dc * 512:(dc + 1) * 512]
            x1b = Bbf[:, 16384 + dc * 512: 16384 + (dc + 1) * 512]
            tld = dma("sync", x1d, y1s[t, dc],
                      waits=[y1_wr[t][dc], x1_free[dc],
                             wo_mm_done if t == 0 else None])
            ta1 = p.op("vector", lambda e, o=Esl(0), a=x1d, b=Abc1(t):
                       e.tensor_mul(o, a, b),
                       waits=[tld, ab_done[t],
                              y_free[0] if (t == 0 and dc == 0) else None,
                              y_free[1] if (t == 0 and dc == 0) else None],
                       inc=sV)
            ta2 = p.op("vector", lambda e, o=x1d, a=Esl(0), b=Bbc1(t):
                       e.tensor_add(o, a, b), inc=sV)
            x1_ready[dc] = ta2
            x1bf_ready[dc] = p.op("vector", lambda e, o=x1b, a=x1d:
                                  e.tensor_copy(o, a), inc=sV)
        for fc in range(NFC):
            buf = w1q_buf(fc)
            par = fc % 2
            if t == 0:
                wtoks = []
                for pc in range(4):
                    dw = [w1q_free[par]] + gwr_w1[par]
                    if fc < 2:
                        dw.append(last_q_mm[15])
                    tqv, tw = quant_piece(
                        "w1", fc, pc, buf[:, pc * 512:(pc + 1) * 512],
                        w1q[fc, :, pc * 512:(pc + 1) * 512], dw)
                    qres["wr_w1"][fc][pc] = tw
                    wtoks.append(tw)
                w1q_tok[par] = tqv
                gwr_w1[par] = wtoks
            else:
                w1q_tok[par] = dma(
                    "sync", buf, w1q[fc],
                    waits=[w1q_free[par]] + qres["wr_w1"][fc] + gwr_w1[par])
                gwr_w1[par] = []
            th = None
            for dc in range(DC):
                th = p.op("tensor",
                          lambda e, o=PB[par].ap(),
                          l=buf[:, dc * 128:(dc + 1) * 128],
                          r=Bbf[:, 16384 + dc * 512: 16384 + (dc + 1) * 512],
                          s=(dc == 0), z=(dc == DC - 1):
                          e.matmul(o, l, r, start=s, stop=z),
                          waits=[w1q_tok[par], x1bf_ready[dc], h_free[par]],
                          inc=sT if dc == DC - 1 else None)
            w1q_free[par] = th
            hb = Aap[:, fc * 512:(fc + 1) * 512]
            hbf_tok[fc] = p.op("scalar",
                               lambda e, o=hb, a=PB[par].ap(),
                               b=b1_sb.ap()[:, fc:fc + 1]:
                               e.activation(o, a, AF.Gelu, bias=b),
                               waits=[th, t_b1], inc=sC)
            h_free[par] = hbf_tok[fc]
        tacc = None
        for dc in range(DC):
            halves = []
            for hh in range(2):
                slot = (dc * 2 + hh) % 3
                sap = w2q_slot(dc * 2 + hh)
                if t == 0:
                    wtoks = []
                    for pc in range(8):
                        dw = [w2q_free[slot]] + gwr_w2[slot]
                        if dc < 2:
                            dw.append(wo_mm_done)
                        tqv, tw = quant_piece(
                            "w2", dc, hh * 8 + pc,
                            sap[:, pc * 512:(pc + 1) * 512],
                            w2q[dc, :, hh * 4096 + pc * 512:
                                hh * 4096 + (pc + 1) * 512], dw)
                        qres["wr_w2"][dc][hh * 8 + pc] = tw
                        wtoks.append(tw)
                    w2q_tok[slot] = tqv
                    gwr_w2[slot] = wtoks
                else:
                    w2q_tok[slot] = dma(
                        "sync", sap, w2q[dc, :, hh * 4096:(hh + 1) * 4096],
                        waits=[w2q_free[slot]] + gwr_w2[slot]
                              + qres["wr_w2"][dc][hh * 8: hh * 8 + 8])
                    gwr_w2[slot] = []
                halves.append((slot, sap, w2q_tok[slot]))
            par = dc % 2
            tmm = None
            for fc in range(NFC):
                slot, sap, wtok = halves[fc // 32]
                tmm = p.op("tensor",
                           lambda e, o=PB[2 + par].ap(),
                           l=sap[:, (fc % 32) * 128:((fc % 32) + 1) * 128],
                           r=Aap[:, fc * 512:(fc + 1) * 512],
                           s=(fc == 0), z=(fc == NFC - 1):
                           e.matmul(o, l, r, start=s, stop=z),
                           waits=[wtok, hbf_tok[fc], out_free[par]],
                           inc=sT if fc % 32 == 31 else None)
                if fc == 31:
                    w2q_free[halves[0][0]] = tmm
            w2q_free[halves[1][0]] = tmm
            x1d = Bfap[:, dc * 512:(dc + 1) * 512]
            tv1 = p.op("vector", lambda e, o=Esl(0), a=PB[2 + par].ap(),
                       b=b2_sb.ap()[:, dc:dc + 1]:
                       e.tensor_scalar(o, a, b, None, ALU.add),
                       waits=[tmm, t_b2], inc=sV)
            out_free[par] = tv1
            tv2 = p.op("vector", lambda e, o=x1d, a=Esl(0), b=x1d:
                       e.tensor_add(o, a, b), waits=[x1_ready[dc]], inc=sV)
            if dc == 0:
                p.op("vector", lambda e, o=S2N: e.memset(o, 0.0),
                     waits=[xres_tok[0] if t == 0 else None], inc=sV)
                p.op("vector", lambda e, o=SQ2N: e.memset(o, 0.0),
                     waits=[xres_tok[1] if t == 0 else None], inc=sV)
            p.op("vector", lambda e, o=S2N, a=x1d:
                 e.tensor_add(o, o, a), waits=[tv2], inc=sV)
            t2s = p.op("scalar", lambda e, o=Esl(1), a=x1d:
                       e.activation(o, a, AF.Square),
                       waits=[tv2, tacc,
                              y_free[1] if (t == 0 and dc == 0) else None],
                       inc=sC)
            tacc = p.op("vector", lambda e, o=SQ2N, a=Esl(1):
                        e.tensor_add(o, o, a), waits=[t2s], inc=sV)
            x1_ready[dc] = (tv2, tacc)
        trs = p.op("tensor", lambda e, o=PB[5].ap()[0:1, :], l=ones_m1.ap(),
                   r=S2N: e.matmul(o, l, r, start=True, stop=True),
                   waits=[tacc, st2_free], inc=sT)
        tm = p.op("vector", lambda e, o=row(0), a=PB[5].ap()[0:1, :]:
                  e.tensor_scalar_mul(o, a, 1.0 / 2048.0),
                  waits=[trs], inc=sV)
        trq = p.op("tensor", lambda e, o=PB[5].ap()[0:1, :], l=ones_m1.ap(),
                   r=SQ2N: e.matmul(o, l, r, start=True, stop=True),
                   waits=[tm], inc=sT)
        te2 = p.op("vector", lambda e, o=row(1), a=PB[5].ap()[0:1, :]:
                   e.tensor_scalar_mul(o, a, 1.0 / 2048.0),
                   waits=[trq], inc=sV)
        st2_free = te2
        tm2 = p.op("scalar", lambda e, o=row(2), a=row(0):
                   e.activation(o, a, AF.Square), waits=[tm], inc=sC)
        tvv = p.op("vector", lambda e, o=row(1), a=row(1), b=row(2):
                   e.tensor_sub(o, a, b), waits=[tm2], inc=sV)
        tl = p.op("scalar", lambda e, o=row(2), a=row(1), b=c_eps_ln.ap():
                  e.activation(o, a, AF.Ln, bias=b), waits=[tvv], inc=sC)
        tA = p.op("scalar", lambda e, o=row(3), a=row(2), s2=c_nhalf.ap():
                  e.activation(o, a, AF.Exp, scale=s2), inc=sC)
        tB1 = p.op("vector", lambda e, o=row(2), a=row(0), b=row(3):
                   e.tensor_mul(o, a, b), waits=[tA, tl], inc=sV)
        p.op("vector", lambda e, o=row(2), a=row(2):
             e.tensor_scalar_mul(o, a, -1.0), inc=sV)
        tsl = p.op("vector", lambda e, o=kv_slk: e.tensor_copy(o, o), inc=sV)
        tba = p.op("tensor", lambda e, o=PB[4].ap(), l=ones_k1.ap(), r=row(3):
                   e.matmul(o, l, r, start=True, stop=True),
                   waits=[tA, bc_free, tsl], inc=sT)
        tca = p.op("vector", lambda e, o=Esl(2), a=PB[4].ap():
                   e.tensor_copy(o, a),
                   waits=[tba, qres["lastB"] if t == 0 else None], inc=sV)
        tbb = p.op("tensor", lambda e, o=PB[4].ap(), l=ones_k1.ap(), r=row(2):
                   e.matmul(o, l, r, start=True, stop=True),
                   waits=[tca, tsl], inc=sT)
        tcb = p.op("vector", lambda e, o=Esl(9), a=PB[4].ap():
                   e.tensor_copy(o, a),
                   waits=[tbb, mx(sst["abs"][0], sst["abs"][1])
                          if t == 0 else None], inc=sV)
        bc_free = tcb
        for dc in range(DC):
            x1d = Bfap[:, dc * 512:(dc + 1) * 512]
            tv2, tstat = x1_ready[dc]
            to1 = p.op("vector", lambda e, o=Esl(0), a=x1d, b=Esl(2):
                       e.tensor_mul(o, a, b), waits=[tca, tstat], inc=sV)
            to2 = p.op("vector", lambda e, o=x1d, a=Esl(0), b=Esl(9):
                       e.tensor_add(o, a, b), waits=[tcb], inc=sV)
            wr = dma("gpsimd",
                     out[dc * 128:(dc + 1) * 128, t * 512:(t + 1) * 512],
                     x1d, waits=[to2])
            x1_free[dc] = wr
            out_last = wr
            x1_ready[dc] = None

    p.op("sync", None, waits=[tok for tok in _q["gpsimd"]["last"]
                              if tok is not None])

    p.replay()
    es.close()
    return nc


_NC_CACHE = {}


def kernel(L0_attn_hidden_states, L0_attn_Wq, L0_attn_bq, L0_attn_Wk,
           L0_attn_bk, L0_attn_Wv, L0_attn_bv, L0_attn_Wo,
           L0_ffn_w1, L0_ffn_b1, L0_ffn_w2, L0_ffn_b2):
    x = np.ascontiguousarray(np.asarray(L0_attn_hidden_states, dtype=np.float32))
    Wq = np.asarray(L0_attn_Wq, dtype=np.float32)
    Wk = np.asarray(L0_attn_Wk, dtype=np.float32)
    Wv = np.asarray(L0_attn_Wv, dtype=np.float32)
    Wo = np.asarray(L0_attn_Wo, dtype=np.float32)
    w1 = np.asarray(L0_ffn_w1, dtype=np.float32)
    w2 = np.asarray(L0_ffn_w2, dtype=np.float32)

    wq_t = np.ascontiguousarray(
        Wq.reshape(16, 128, 16, 128).transpose(0, 3, 2, 1)).reshape(16, 128, 2048)
    wo_t = np.ascontiguousarray(
        Wo.reshape(16, 128, 16, 128).transpose(0, 3, 2, 1)).reshape(16, 128, 2048)
    wk_t = np.ascontiguousarray(Wk.T)
    wv_t = np.ascontiguousarray(Wv.T)
    w1_t = np.ascontiguousarray(
        w1.reshape(64, 128, 16, 128).transpose(0, 3, 2, 1)).reshape(64, 128, 2048)
    w2_t = np.ascontiguousarray(
        w2.reshape(16, 128, 64, 128).transpose(0, 3, 2, 1)).reshape(16, 128, 8192)
    bq_t = np.ascontiguousarray(
        np.asarray(L0_attn_bq, dtype=np.float32).reshape(16, 128).T)
    b1_t = np.ascontiguousarray(
        np.asarray(L0_ffn_b1, dtype=np.float32).reshape(64, 128).T)
    b2_t = np.ascontiguousarray(
        np.asarray(L0_ffn_b2, dtype=np.float32).reshape(16, 128).T)
    bk_t = np.ascontiguousarray(np.asarray(L0_attn_bk, dtype=np.float32)[None, :])
    bv_t = np.ascontiguousarray(np.asarray(L0_attn_bv, dtype=np.float32)[None, :])

    in_maps = []
    for c in range(8):
        b, half = c // 2, c % 2
        xtb = x[b].T
        own = xtb[:, half * 2048:(half + 1) * 2048]
        other = xtb[:, (1 - half) * 2048:(2 - half) * 2048]
        xt_c = np.ascontiguousarray(np.concatenate([own, other], axis=1))
        in_maps.append({
            "xt": xt_c, "wq": wq_t, "wk": wk_t, "wv": wv_t, "wo": wo_t,
            "w1": w1_t, "w2": w2_t, "bq": bq_t, "bk": bk_t, "bv": bv_t,
            "b1": b1_t, "b2": b2_t,
        })

    key = "dbg" if _DEBUG else "nc"
    if key not in _NC_CACHE:
        _NC_CACHE[key] = build_nc()
    res = run_bass_kernel_spmd(_NC_CACHE[key], in_maps, core_ids=list(range(8)))

    outp = np.empty((4, 4096, 2048), dtype=np.float32)
    for c in range(8):
        b, half = c // 2, c % 2
        outp[b, half * 2048:(half + 1) * 2048, :] = res.results[c]["out"].T
    if _DEBUG:
        return outp, res.results
    return outp
